# version 28
# baseline (speedup 1.0000x reference)
import numpy as np
import ml_dtypes
import concourse.bass as bass
import concourse.mybir as mybir
from concourse.bass_utils import run_bass_kernel_spmd
from contextlib import ExitStack

F32 = mybir.dt.float32
BF16 = mybir.dt.bfloat16
I32 = mybir.dt.int32
ALU = mybir.AluOpType
AF = mybir.ActivationFunctionType
AX = mybir.AxisListType.X

NCORES = 8
SEQ = 2048
D = 1024
NH = 16
DH = 64
E = 8
FD = 4096
T = 2 * SEQ
EPS = 1e-5
TWO_PI = float(2.0 * np.pi)
INV2PI = float(1.0 / (2.0 * np.pi))
HALF_PI = float(np.pi / 2.0)

ENGS = ("sync", "scalar", "vector", "tensor", "gpsimd")


class Sm:
    def __init__(self, h):
        self.h = h
        self.n = 0

    def bump(self, k=1):
        self.n += k
        return self.n


def build_nc(debug=False, audit=None):
    nc = bass.Bass("TRN2", target_bir_lowering=False)

    x_c = nc.dram_tensor("x_c", [T, D], F32, kind="ExternalInput")
    wq_f = nc.dram_tensor("wq_f", [D, 4 * D], F32, kind="ExternalInput")
    wout_f = nc.dram_tensor("wout_f", [D, D], F32, kind="ExternalInput")
    rw_f = nc.dram_tensor("rw_f", [D, E], F32, kind="ExternalInput")
    w1_f = nc.dram_tensor("w1_f", [E, D, FD], F32, kind="ExternalInput")
    w2_f = nc.dram_tensor("w2_f", [E, FD, D], F32, kind="ExternalInput")
    bq_row_d = nc.dram_tensor("bq_row", [1, 4 * D], F32, kind="ExternalInput")
    bo_row_d = nc.dram_tensor("bo_row", [1, D], F32, kind="ExternalInput")
    th_row_d = nc.dram_tensor("th_row", [1, D], F32, kind="ExternalInput")
    thcol_d = nc.dram_tensor("thcol", [128, 8], F32, kind="ExternalInput")
    lnw_row_d = nc.dram_tensor("lnw_row", [1, D], F32, kind="ExternalInput")
    lnb_row_d = nc.dram_tensor("lnb_row", [1, D], F32, kind="ExternalInput")
    sc_row_d = nc.dram_tensor("sc_row", [1, E], F32, kind="ExternalInput")
    eye_d = nc.dram_tensor("eye_d", [128, 128], BF16, kind="ExternalInput")
    out_c = nc.dram_tensor("out_c", [T, D], F32, kind="ExternalOutput")

    xnsT = nc.dram_tensor("xnsT", [D, T], BF16, kind="Internal")
    wqs = nc.dram_tensor("wqs", [D, 4 * D], BF16, kind="Internal")
    wouts = nc.dram_tensor("wouts", [D, D], BF16, kind="Internal")
    rws = nc.dram_tensor("rws", [D, E], BF16, kind="Internal")
    w1s = nc.dram_tensor("w1s", [E, D, FD], BF16, kind="Internal")
    w2s = nc.dram_tensor("w2s", [E, FD, D], BF16, kind="Internal")
    if debug:
        xns_d = nc.dram_tensor("xns_d", [D, T], BF16, kind="ExternalOutput")
        kv_d = nc.dram_tensor("kv_d", [128, 2048], F32, kind="ExternalOutput")
        A_d = nc.dram_tensor("A_d", [128, 64], F32, kind="ExternalOutput")
        att_d = nc.dram_tensor("att_d", [128, 8192], F32, kind="ExternalOutput")
        q_d = nc.dram_tensor("q_d", [128, 8192], F32, kind="ExternalOutput")
        xnt_d = nc.dram_tensor("xnt_d", [128, 8192], BF16, kind="ExternalOutput")
        iob_d = nc.dram_tensor("iob_d", [128, 1024], F32, kind="ExternalOutput")
        tab_d = nc.dram_tensor("tab_d", [128, 2048], F32, kind="ExternalOutput")
        qs_d = nc.dram_tensor("qs_d", [128, 512], F32, kind="ExternalOutput")
        tb1_d = nc.dram_tensor("tb1_d", [128, 2048], F32, kind="ExternalOutput")
        kr_d = nc.dram_tensor("kr_d", [128, 2048], F32, kind="ExternalOutput")
        kd_d = nc.dram_tensor("kd_d", [128, 1024], F32, kind="ExternalOutput")
        vd_d = nc.dram_tensor("vd_d", [128, 1024], F32, kind="ExternalOutput")
        lg_d = nc.dram_tensor("lg_d", [128, 8], F32, kind="ExternalOutput")
        rw_d = nc.dram_tensor("rw_d", [128, 64], F32, kind="ExternalOutput")
        rchain_d = nc.dram_tensor("rchain_d", [128, 64], F32, kind="ExternalOutput")

    es = ExitStack()
    A = es.enter_context

    acc = A(nc.sbuf_tensor("acc", [128, 8 * D], F32))
    xnT = A(nc.sbuf_tensor("xnT", [128, 8 * D], BF16))
    big1 = A(nc.sbuf_tensor("big1", [128, 16 * 512], BF16))
    w1b = A(nc.sbuf_tensor("w1b", [128, 8 * 2048], BF16))
    w2b = A(nc.sbuf_tensor("w2b", [128, 16 * 1024], BF16))
    kvb = A(nc.sbuf_tensor("kvb", [128, 2048], BF16))
    usb = A(nc.sbuf_tensor("usb", [128, 4096], BF16))
    gbf = A(nc.sbuf_tensor("gbf", [128, 2048], BF16))
    gT = A(nc.sbuf_tensor("gT", [128, 2048], BF16))
    eye = A(nc.sbuf_tensor("eye", [128, 128], BF16))
    rw = A(nc.sbuf_tensor("rw", [128, 64], BF16))

    attsb = A(nc.sbuf_tensor("attsb", [128, 2048], F32))
    fsc1 = A(nc.sbuf_tensor("fsc1", [128, 1024], F32))
    fsc2 = A(nc.sbuf_tensor("fsc2", [128, 1024], F32))
    ksb = A(nc.sbuf_tensor("ksb", [128, 1024], F32))
    tbarg = A(nc.sbuf_tensor("tbarg", [128, 1024], F32))
    tbsin = A(nc.sbuf_tensor("tbsin", [128, 1024], BF16))
    tbcos = A(nc.sbuf_tensor("tbcos", [128, 1024], BF16))
    shbc1 = A(nc.sbuf_tensor("shbc1", [128, 1024], F32))
    lnbbc = A(nc.sbuf_tensor("lnbbc", [128, 1024], F32))
    iotabc = A(nc.sbuf_tensor("iotabc", [128, 1024], F32))
    psc = A(nc.sbuf_tensor("psc", [128, 1024], F32))
    ki = A(nc.sbuf_tensor("ki", [128, 512], I32))
    stats = A(nc.sbuf_tensor("stats", [128, 24], F32))
    mv = A(nc.sbuf_tensor("mv", [128, 4], F32))
    cols = A(nc.sbuf_tensor("cols", [128, 16], F32))
    Asb = A(nc.sbuf_tensor("Asb", [128, 64], F32))
    r8a = A(nc.sbuf_tensor("r8a", [128, 8], F32))
    r8b = A(nc.sbuf_tensor("r8b", [128, 8], F32))
    r8c = A(nc.sbuf_tensor("r8c", [128, 8], F32))
    sbc = A(nc.sbuf_tensor("sbc", [128, 8], F32))
    thcol = A(nc.sbuf_tensor("thcol_s", [128, 8], F32))
    poscol = A(nc.sbuf_tensor("poscol", [128, 16], F32))
    bq_row = A(nc.sbuf_tensor("bq_row_s", [1, 4 * D], BF16))
    bo_row = A(nc.sbuf_tensor("bo_row_s", [1, D], BF16))
    ones_r = A(nc.sbuf_tensor("ones_r", [1, 128], F32))
    row_s = attsb[0:1, 0:1024]
    ones_bf = A(nc.sbuf_tensor("ones_bf", [1, 512], BF16))
    krot = usb[:, 0:2048]
    vsb = usb[:, 2048:3072]
    vsb2 = usb[:, 3072:4096]
    thetabc = shbc1
    lnwbc = shbc1

    pb = [A(nc.psum_tensor(f"pb{i}", [128, 512], F32)) for i in range(7)]
    ptb = A(nc.psum_tensor("ptb", [128, 1024], BF16))

    QS = Sm(A(nc.semaphore("qs")))
    QW = Sm(A(nc.semaphore("qw")))
    SC = Sm(A(nc.semaphore("sc")))
    KL = Sm(A(nc.semaphore("kl")))
    XS = Sm(A(nc.semaphore("xs")))
    PE = Sm(A(nc.semaphore("pe")))
    VE = Sm(A(nc.semaphore("ve")))
    AE = Sm(A(nc.semaphore("ae")))
    GE = Sm(A(nc.semaphore("ge")))

    P = {e: [] for e in ENGS}
    WAITED = {e: {} for e in ENGS}

    def _op(eng, build, waits=(), inc=None, inck=1):
        wl = []
        for w in waits:
            if w is None:
                continue
            sem, tgt = w
            if tgt is None or tgt <= 0:
                continue
            prev = WAITED[eng].get(id(sem), 0)
            if tgt > prev:
                WAITED[eng][id(sem)] = tgt
                wl.append((sem.h, tgt))
        ih = None
        if inc is not None:
            inc.bump(inck)
            ih, iv = inc.h, inck
        if audit is not None:
            audit.append((eng, [(id(s_), t_) for s_, t_ in
                                [(w[0], w[1]) for w in waits
                                 if w is not None and w[1] and w[1] > 0]],
                          (id(inc), inck) if inc is not None else None,
                          len(audit)))

        def f(eng_obj):
            for h, t in wl:
                eng_obj.wait_ge(h, t)
            ins = build(eng_obj)
            if ih is not None:
                ins.then_inc(ih, iv)
        P[eng].append(f)
        return inc.n if inc is not None else None

    def DMA(eng, qsem, out_ap, in_ap, waits=(), transpose=False, **kw):
        _op(eng, lambda o: o.dma_start(out_ap, in_ap, transpose=transpose, **kw),
            waits, inc=qsem, inck=16)
        return qsem.n

    def MM(out, l, r, start, stop, waits=(), inc=None, skip=False):
        return _op("tensor",
                   lambda t: t.matmul(out, l, r, start=start, stop=stop,
                                      skip_group_check=skip),
                   waits, inc)

    def TRANS(out, in_, waits=(), inc=None):
        return _op("tensor", lambda t: t.transpose(out, in_, eye[:]), waits, inc)

    def ACT(out, in_, func, bias=None, scale=None, accum=None, waits=(), inc=None):
        def b(s_):
            kw = {}
            if bias is not None:
                kw["bias"] = bias
            if scale is not None:
                kw["scale"] = scale
            if accum is not None:
                kw["accum_out"] = accum
            return s_.activation(out, in_, func, **kw)
        return _op("scalar", b, waits, inc)

    def V(fn, waits=(), inc=None):
        w = list(waits) + [(VE, VE.n)]
        return _op("vector", fn, w, inc=VE)

    def G(fn, waits=(), inc=None):
        w = list(waits) + [(GE, GE.n)]
        return _op("gpsimd", fn, w, inc=GE)

    bankfree = {}

    def bfree(bank):
        return bankfree.get(id(bank))

    def bset(bank, semcount):
        bankfree[id(bank)] = semcount

    lastuse = {}

    DMA("sync", QS, eye[:], eye_d[:])
    DMA("sync", QS, thcol[:], thcol_d[:])
    q_throw = DMA("sync", QS, row_s, th_row_d[:])
    consts_q = QS.n
    DMA("gpsimd", SC, bq_row[:], bq_row_d[:])
    DMA("gpsimd", SC, bo_row[:], bo_row_d[:])
    sc_bias = SC.n

    for pc in range(16):
        G(lambda g, pc=pc: g.iota(poscol[:, pc:pc + 1], [[1, 1]], base=pc * 128,
                                  channel_multiplier=1,
                                  allow_small_or_imprecise_dtypes=True))
    G(lambda g: g.iota(iotabc[:], [[1, 1024]], base=0,
                       channel_multiplier=0,
                       allow_small_or_imprecise_dtypes=True))
    G(lambda g: g.engine_nop(), inc=GE)
    ge_iota = GE.n

    DMA("gpsimd", SC, rws[:], rw_f[:], max_dma_last_dim=8192)
    sc_router = SC.n
    DMA("gpsimd", SC, wqs[:, 0:D], wq_f[:, 0:D], max_dma_last_dim=8192)
    DMA("gpsimd", SC, wqs[:, 3 * D:4 * D], wq_f[:, 3 * D:4 * D],
        max_dma_last_dim=8192)
    sc_qkvu = SC.n
    DMA("gpsimd", SC, wouts[:], wout_f[:], max_dma_last_dim=8192)
    sc_wout = SC.n
    sc_w1 = {}
    sc_w2 = {}
    for e_ in range(E):
        for h_ in range(2):
            DMA("gpsimd", SC, w1s[e_, :, h_ * 2048:(h_ + 1) * 2048],
                w1_f[e_, :, h_ * 2048:(h_ + 1) * 2048], max_dma_last_dim=8192)
            sc_w1[(e_, h_)] = SC.n
            DMA("gpsimd", SC, w2s[e_, h_ * 2048:(h_ + 1) * 2048, :],
                w2_f[e_, h_ * 2048:(h_ + 1) * 2048, :], max_dma_last_dim=8192)
            sc_w2[(e_, h_)] = SC.n

    V(lambda v: v.memset(ones_r[:], 1.0))
    V(lambda v: v.memset(ones_bf[:], 1.0))
    V(lambda v: v.memset(cols[:, 0:1], EPS))
    V(lambda v: v.memset(cols[:, 1:2], HALF_PI))
    V(lambda v: v.memset(cols[:, 11:12], -1.0), inc=VE)
    ve_memset = VE.n

    def bcast_row(dst_sbuf, src_row_ap, ncols, waits_pe):
        last = None
        for j in range(0, ncols, 512):
            bank = pb[2 + (j // 512) % 2]
            MM(bank[:, 0:512], ones_r[0:1, 0:128], src_row_ap[0:1, j:j + 512],
               True, True, waits=list(waits_pe) + [bfree(bank)], inc=PE)
            pe_n = PE.n
            V(lambda v, b=bank, j=j: v.tensor_copy(dst_sbuf[:, j:j + 512], b[:, 0:512]),
              waits=[(PE, pe_n)], inc=VE)
            bset(bank, (VE, VE.n))
            last = VE.n
        return last

    ve_thetabc = bcast_row(thetabc, row_s, D, [(QS, q_throw), (VE, ve_memset)])
    q_sc = DMA("sync", QS, lnbbc[0:1, 0:E], sc_row_d[:])
    bank = pb[2]
    MM(bank[:, 0:8], ones_r[0:1, 0:128], lnbbc[0:1, 0:E], True, True,
       waits=[(QS, q_sc), bfree(bank)], inc=PE)
    pe_sbc = PE.n
    V(lambda v, b=bank: v.tensor_copy(sbc[:], b[:, 0:8]), waits=[(PE, pe_sbc)], inc=VE)
    bset(bank, (VE, VE.n))
    ve_sbc = VE.n
    lastuse[("xa", 0)] = (PE, PE.n)

    DMA("scalar", KL, acc[:].rearrange("p (c n) -> p c n", c=8),
        wq_f[:, D:2 * D].rearrange("(c p) n -> p c n", p=128))
    kl_k = KL.n
    ACT(w1b[:, 0:8192], acc[:], AF.Copy, waits=[(KL, kl_k)], inc=AE)
    ae_kvk = AE.n
    DMA("scalar", KL, acc[:].rearrange("p (c n) -> p c n", c=8),
        wq_f[:, 2 * D:3 * D].rearrange("(c p) n -> p c n", p=128),
        waits=[(AE, ae_kvk)])
    kl_v = KL.n
    ACT(w1b[:, 8192:16384], acc[:], AF.Copy, waits=[(KL, kl_v)], inc=AE)
    ae_kvv = AE.n
    lastuse[("acc",)] = (AE, ae_kvv)

    qw_rw = DMA("scalar", QW, rw[:].rearrange("p (c e) -> p c e", e=8),
                rws[:].rearrange("(c p) e -> p c e", p=128), waits=[(SC, sc_router)])

    xstore_q = [0] * 32
    kv_first = [True] * 4
    tabslot_use = {}

    def p1_tables(pc):
        s = pc % 2
        sinb = tbsin[:] if s == 0 else big1[:, 0:1024]
        cosb = tbcos[:] if s == 0 else big1[:, 1024:2048]
        V(lambda v, pc=pc: v.tensor_scalar(tbarg[:], thetabc[:], poscol[:, pc:pc + 1],
                                           None, ALU.mult),
          waits=[(VE, ve_thetabc), (GE, ge_iota), lastuse.get(("tbA",))])
        V(lambda v: v.tensor_scalar(fsc1[:, 0:512], tbarg[:, 0:512], INV2PI,
                                    None, ALU.mult))
        V(lambda v: v.tensor_copy(ki[:], fsc1[:, 0:512]))
        V(lambda v: v.tensor_copy(fsc1[:, 0:512], ki[:]))
        V(lambda v: v.scalar_tensor_tensor(fsc2[:, 0:512], fsc1[:, 0:512], -TWO_PI,
                                           tbarg[:, 0:512], ALU.mult, ALU.add))
        V(lambda v: v.tensor_scalar(fsc1[:, 0:512], tbarg[:, 0:512], INV2PI, 0.25,
                                    ALU.mult, ALU.add))
        V(lambda v: v.tensor_copy(ki[:], fsc1[:, 0:512]))
        V(lambda v: v.tensor_copy(fsc1[:, 0:512], ki[:]))
        V(lambda v: v.scalar_tensor_tensor(fsc2[:, 512:1024], fsc1[:, 0:512], -TWO_PI,
                                           tbarg[:, 0:512], ALU.mult, ALU.add),
          inc=VE)
        ve_args = VE.n
        sw = tabslot_use.get((s, "sin"))
        cw = tabslot_use.get((s, "cos"))
        ACT(sinb[:, 0:512], fsc2[:, 0:512], AF.Sin, waits=[(VE, ve_args), sw, cw])
        ACT(cosb[:, 0:512], fsc2[:, 512:1024], AF.Sin, bias=cols[:, 1:2])
        ACT(sinb[:, 512:1024], tbarg[:, 512:1024], AF.Sin)
        ACT(cosb[:, 512:1024], tbarg[:, 512:1024], AF.Sin,
            bias=cols[:, 1:2], inc=AE)
        lastuse[("tbA",)] = (AE, AE.n)
        return sinb, cosb, AE.n

    tiles = [(pc, sl) for pc in range(16) for sl in range(2)]
    q_x = [0] * 32

    def p1_load(i):
        pc, sl = tiles[i]
        tr = sl * SEQ + pc * 128
        slot = i % 2
        xa = attsb[:, slot * 1024:(slot + 1) * 1024]
        q_x[i] = DMA("sync", QS, xa, x_c[tr:tr + 128, :],
                     waits=[lastuse.get(("xa", slot))])

    def p1_ln(i, tab):
        pc, sl = tiles[i]
        tc = sl * 16 + pc
        tr = sl * SEQ + pc * 128
        slot = i % 2
        xa = attsb[:, slot * 1024:(slot + 1) * 1024]
        qx = q_x[i]
        st = stats[:, slot * 12:(slot + 1) * 12]
        mvs = mv[:, slot * 2:(slot + 1) * 2]
        V(lambda v, xa=xa, st=st: v.bn_stats(st[:, 0:6], xa[:, 0:512]),
          waits=[(QS, qx)])
        V(lambda v, xa=xa, st=st: v.bn_stats(st[:, 6:12], xa[:, 512:1024]), inc=VE)
        ve_bs = VE.n
        V(lambda v, st=st, mvs=mvs: v.bn_aggr(mvs, st), waits=[(VE, ve_bs)], inc=VE)
        ve_mv = VE.n
        rsd = cols[:, 2 + slot:3 + slot]
        rsd2 = cols[:, 4 + slot:5 + slot]
        ACT(rsd, mvs[:, 1:2], AF.Sqrt, bias=cols[:, 0:1], waits=[(VE, ve_mv)], inc=AE)
        ae_s = AE.n
        V(lambda v, rsd=rsd, rsd2=rsd2: v.reciprocal(rsd2, rsd),
          waits=[(AE, ae_s)], inc=VE)
        ve_r = VE.n
        xw = gbf[:, slot * 1024:(slot + 1) * 1024]
        V(lambda v, xa=xa, mvs=mvs, rsd2=rsd2, xw=xw:
          v.tensor_scalar(xw, xa, mvs[:, 0:1], rsd2, ALU.subtract, ALU.mult),
          waits=[(VE, ve_r), lastuse.get(("gbf", slot))], inc=VE)
        ve_xn = VE.n
        lastuse[("xa", slot)] = (VE, ve_xn)
        lastuse[("attsb", slot)] = (VE, ve_xn)
        return dict(pc=pc, sl=sl, slot=slot, tc=tc, tr=tr, ve_xn=ve_xn, xw=xw,
                    tab=tab)

    def p1_rest(stt_):
        pc, sl, slot = stt_["pc"], stt_["sl"], stt_["slot"]
        tc, tr = stt_["tc"], stt_["tr"]
        xw, ve_xn = stt_["xw"], stt_["ve_xn"]
        sinb, cosb, ae_tab1 = stt_["tab"]
        s = pc % 2
        xp = gT[:, slot * 1024:(slot + 1) * 1024]
        for c in range(8):
            TRANS(ptb[:, c * 128:(c + 1) * 128], xw[:, c * 128:(c + 1) * 128],
                  waits=([(VE, ve_xn), (QS, consts_q), lastuse.get(("ptb",))]
                         if c == 0 else []),
                  inc=(PE if c == 7 else None))
        pe_tp = PE.n
        lastuse[("gbf", slot)] = (PE, pe_tp)
        ACT(xp, ptb[:], AF.Copy, waits=[(PE, pe_tp), lastuse.get(("gTa", slot)),
                                        lastuse.get(("gTq", slot))],
            inc=AE)
        ae_xp = AE.n
        lastuse[("ptb",)] = (AE, ae_xp)
        qst = DMA("scalar", XS,
                  xnsT.rearrange("(c p) t -> p c t", p=128)[:, :, tr:tr + 128],
                  xp.rearrange("p (c t) -> p c t", t=128),
                  waits=[(AE, ae_xp)])
        xstore_q[tc] = qst
        lastuse[("gTq", slot)] = (XS, qst)
        ksl = ksb if slot == 0 else psc
        vsl = vsb if slot == 0 else vsb2
        pe_k = [0, 0]
        for half in range(2):
            bank = pb[half]
            for c in range(8):
                MM(bank[:, 0:512], xp[:, c * 128:(c + 1) * 128],
                   w1b[:, c * 1024 + half * 512: c * 1024 + half * 512 + 512],
                   c == 0, False,
                   waits=([(AE, ae_xp), (AE, ae_kvk), bfree(bank)]
                          if c == 0 else []))
            MM(bank[:, 0:512], ones_bf[0:1, 0:128],
               bq_row[0:1, D + half * 512: D + half * 512 + 512],
               False, True, waits=[(QS, consts_q), (SC, sc_bias)], inc=PE)
            pe_k[half] = PE.n
        for half in range(2):
            bank = pb[half]
            ACT(ksl[:, half * 512:(half + 1) * 512], bank[:, 0:512], AF.Silu,
                waits=[(PE, pe_k[half]), lastuse.get(("ksb", slot))], inc=AE)
            bset(bank, (AE, AE.n))
        ae_k = AE.n
        V(lambda v, cosb=cosb, ksl=ksl: v.tensor_mul(fsc1[:], ksl[:], cosb),
          waits=[(AE, max(ae_k, ae_tab1)), lastuse.get(("krot",)),
                 lastuse.get(("tbA",))])
        V(lambda v, sinb=sinb, ksl=ksl: v.tensor_mul(fsc2[:], ksl[:], sinb))
        kr4 = krot[:].rearrange("p (h two dh) -> p h two dh", two=2, dh=DH)
        V(lambda v, kr4=kr4: v.tensor_copy(
            kr4[:, :, 0, :], fsc1[:].rearrange("p (h dh) -> p h dh", dh=DH)))
        V(lambda v, kr4=kr4: v.tensor_copy(
            kr4[:, :, 1, :], fsc2[:].rearrange("p (h dh) -> p h dh", dh=DH)),
          inc=VE)
        ve_krot = VE.n
        lastuse[("ksb", slot)] = (VE, ve_krot)
        tabslot_use[(s, "sin")] = (VE, ve_krot)
        tabslot_use[(s, "cos")] = (VE, ve_krot)
        pe_v = [0, 0]
        for half in range(2):
            bank = pb[half]
            for c in range(8):
                MM(bank[:, 0:512], xp[:, c * 128:(c + 1) * 128],
                   w1b[:, 8192 + c * 1024 + half * 512:
                       8192 + c * 1024 + half * 512 + 512],
                   c == 0, False,
                   waits=([(AE, ae_kvv), bfree(bank)] if c == 0 else []))
            MM(bank[:, 0:512], ones_bf[0:1, 0:128],
               bq_row[0:1, 2 * D + half * 512: 2 * D + half * 512 + 512],
               False, True, inc=PE)
            pe_v[half] = PE.n
        for half in range(2):
            bank = pb[half]
            ACT(vsl[:, half * 512:(half + 1) * 512], bank[:, 0:512], AF.Copy,
                waits=[(PE, pe_v[half]), lastuse.get(("vsb", slot))], inc=AE)
            bset(bank, (AE, AE.n))
        ae_v = AE.n
        for h in range(NH):
            kvbank = pb[3 + sl * 2 + h // 8]
            bi = sl * 2 + h // 8
            first = kv_first[bi]
            kv_first[bi] = False
            w = []
            if h == 0:
                w = [(VE, ve_krot), (AE, ae_v)]
                if first:
                    w += [bfree(pb[3 + sl * 2]), bfree(pb[4 + sl * 2])]
            MM(kvbank[:, (h % 8) * 64:(h % 8) * 64 + 64],
               krot[:, h * 128:(h + 1) * 128], vsl[:, h * 64:(h + 1) * 64],
               first, pc == 15, waits=w, skip=True,
               inc=(PE if h == NH - 1 else None))
        lastuse[("krot",)] = (PE, PE.n)
        lastuse[("vsb", slot)] = (PE, PE.n)
        lastuse[("gTa", slot)] = (PE, PE.n)

    p1_load(0)
    p1_load(1)
    tabs = {0: p1_tables(0)}
    lns = [None] * 32
    lns[0] = p1_ln(0, tabs[0])
    qw_qu0 = None
    for i in range(32):
        if i + 2 < 32:
            p1_load(i + 2)
        if i + 1 < 32:
            if (i + 1) % 2 == 0:
                tabs[(i + 1) // 2] = p1_tables((i + 1) // 2)
            lns[i + 1] = p1_ln(i + 1, tabs[(i + 1) // 2])
        p1_rest(lns[i])
    pe_kv_done = PE.n
    lastuse[("gT", 0)] = (XS, xstore_q[15])
    lastuse[("gT", 1)] = (XS, xstore_q[31])
    for sl in range(2):
        for bi in range(2):
            bank = pb[3 + sl * 2 + bi]
            ACT(kvb[:, sl * 1024 + bi * 512: sl * 1024 + bi * 512 + 512],
                bank[:, 0:512], AF.Copy, waits=[(PE, pe_kv_done)], inc=AE)
            bset(bank, (AE, AE.n))
    ae_kv = AE.n
    lastuse[("qs", 0)] = (VE, VE.n)
    lastuse[("qs", 1)] = (VE, VE.n)
    lastuse[("big1",)] = (AE, AE.n)
    for _ts in range(4):
        lastuse[("usb", _ts)] = (PE, pe_kv_done)
    q_lnw = DMA("sync", QS, row_s, lnw_row_d[:], waits=[(PE, PE.n), (VE, VE.n)])
    ve_lnwbc = bcast_row(lnwbc, row_s, D, [(QS, q_lnw), (VE, VE.n), (AE, ae_kv)])
    q_lnb = DMA("sync", QS, row_s, lnb_row_d[:], waits=[(PE, PE.n)])
    ve_lnbbc = bcast_row(lnbbc, row_s, D, [(QS, q_lnb)])
    lastuse[("attsb", 0)] = (PE, PE.n)
    if debug:
        DMA("sync", QS, xns_d[:], xnsT[:], waits=[(XS, max(xstore_q))])
        V(lambda v: v.tensor_copy(attsb[:], big1[:, 0:2048]),
          waits=[(PE, pe_kv_done)], inc=VE)
        q_tb1 = DMA("sync", QS, tb1_d[:], attsb[:], waits=[(VE, VE.n)])
        V(lambda v: v.tensor_copy(attsb[:], krot[:]),
          waits=[(QS, q_tb1)], inc=VE)
        q_tb1 = DMA("sync", QS, kr_d[:], attsb[:], waits=[(VE, VE.n)])
        V(lambda v: v.tensor_copy(attsb[:, 0:1024], psc[:]),
          waits=[(QS, q_tb1)])
        V(lambda v: v.tensor_copy(attsb[:, 1024:2048], vsb2[:]), inc=VE)
        q_tb1 = DMA("sync", QS, kd_d[:], attsb[:, 0:1024], waits=[(VE, VE.n)])
        q_tb1 = DMA("sync", QS, vd_d[:], attsb[:, 1024:2048])
        lastuse[("attsb", 0)] = (QS, q_tb1)
        lastuse[("attsb", 1)] = (QS, q_tb1)

    pe_last_w1b_use = (PE, pe_kv_done)
    pe_last_w2b_use = None
    qw_attnw = {}
    qw_w1 = {}
    qw_w2 = {}
    acc_store_q = 0
    acc_last_ve = [0] * 16
    prev_ae_ep = 0

    def load_attn_qu(g):
        DMA("scalar", QW, w1b[:, 0:8192].rearrange("p (c n) -> p c n", c=8),
            wqs[:, 0:D].rearrange("(c p) n -> p c n", p=128),
            waits=[pe_last_w1b_use, (SC, sc_qkvu)])
        last = DMA("scalar", QW,
                   w1b[:, 8192:16384].rearrange("p (c n) -> p c n", c=8),
                   wqs[:, 3 * D:4 * D].rearrange("(c p) n -> p c n", p=128))
        qw_attnw[(g, "qu")] = last

    def load_attn_wo(g):
        last = DMA("scalar", QW,
                   w2b[:, 0:8192].rearrange("p (c n) -> p c n", c=8),
                   wouts[:].rearrange("(c p) n -> p c n", p=128),
                   waits=[pe_last_w2b_use, (SC, sc_wout)])
        qw_attnw[(g, "wo")] = last

    def load_w1(g, e_, h_):
        last = DMA("scalar", QW, w1b[:].rearrange("p (c n) -> p c n", c=8),
                   w1s[e_, :, h_ * 2048:(h_ + 1) * 2048]
                   .rearrange("(c p) n -> p c n", p=128),
                   waits=[pe_last_w1b_use, (SC, sc_w1[(e_, h_)])])
        qw_w1[(g, e_, h_)] = last

    def load_w2(g, e_, h_):
        last = DMA("scalar", QW, w2b[:].rearrange("p (fc n) -> p fc n", fc=16),
                   w2s[e_, h_ * 2048:(h_ + 1) * 2048, :]
                   .rearrange("(fc p) n -> p fc n", p=128),
                   waits=[pe_last_w2b_use, (SC, sc_w2[(e_, h_)])])
        qw_w2[(g, e_, h_)] = last

    load_attn_qu(0)
    load_attn_wo(0)

    for g in range(4):
        sl, hf = g // 2, g % 2
        tb = sl * SEQ + hf * 1024
        qub = w1b
        wob = w2b
        iot = iotabc
        ge_iotabc = ge_iota
        ve_iot = None
        if g > 0:
            shift = 1024.0 if hf == 1 else -1024.0
            V(lambda v, shift=shift: v.tensor_scalar(
                iotabc[:], iotabc[:], shift, None, ALU.add),
              waits=[(GE, ge_iota), lastuse.get(("iotabc",))], inc=VE)
            ve_iot = VE.n

        def p2_tables(nt, m):
            s2 = m % 2
            sinb = tbsin[:, s2 * 512:(s2 + 1) * 512]
            cosb = tbcos[:, s2 * 512:(s2 + 1) * 512]
            sw = tabslot_use.get(("p2", s2))
            if m >= 3:
                V(lambda v, nt=nt, m=m: v.tensor_scalar(
                    fsc2[:, 0:512], iot[:, nt * 512: nt * 512 + 512],
                    thcol[:, m:m + 1], None, ALU.mult),
                  waits=[(GE, ge_iotabc), (VE, ve_iot) if ve_iot else None,
                         (QS, consts_q), lastuse.get(("tbA2",))])
                V(lambda v: v.tensor_scalar(fsc2[:, 512:1024], fsc2[:, 0:512],
                                            -1.0, None, ALU.mult), inc=VE)
                ve_a = VE.n
                ACT(sinb, fsc2[:, 0:512], AF.Sin, waits=[(VE, ve_a), sw])
                ACT(cosb, fsc2[:, 512:1024], AF.Sin, bias=cols[:, 1:2], inc=AE)
                lastuse[("tbA2",)] = (AE, AE.n)
                return sinb, cosb, AE.n
            V(lambda v, nt=nt, m=m: v.tensor_scalar(
                tbarg[:, 0:512], iot[:, nt * 512: nt * 512 + 512],
                thcol[:, m:m + 1], None, ALU.mult),
              waits=[(GE, ge_iotabc), (VE, ve_iot) if ve_iot else None,
                     (QS, consts_q), lastuse.get(("tbA2",))])
            V(lambda v: v.tensor_scalar(fsc1[:, 0:512], tbarg[:, 0:512], INV2PI,
                                        None, ALU.mult))
            V(lambda v: v.tensor_copy(ki[:], fsc1[:, 0:512]))
            V(lambda v: v.tensor_copy(fsc1[:, 0:512], ki[:]))
            V(lambda v: v.scalar_tensor_tensor(fsc2[:, 0:512], fsc1[:, 0:512],
                                               -TWO_PI, tbarg[:, 0:512],
                                               ALU.mult, ALU.add))
            V(lambda v: v.tensor_scalar(fsc1[:, 0:512], tbarg[:, 0:512], INV2PI,
                                        0.25, ALU.mult, ALU.add))
            V(lambda v: v.tensor_copy(ki[:], fsc1[:, 0:512]))
            V(lambda v: v.tensor_copy(fsc1[:, 0:512], ki[:]))
            V(lambda v: v.scalar_tensor_tensor(fsc2[:, 512:1024], fsc1[:, 0:512],
                                               -TWO_PI, tbarg[:, 0:512],
                                               ALU.mult, ALU.add), inc=VE)
            ve_args = VE.n
            ACT(sinb, fsc2[:, 0:512], AF.Sin, waits=[(VE, ve_args), sw])
            ACT(cosb, fsc2[:, 512:1024], AF.Sin, bias=cols[:, 1:2], inc=AE)
            lastuse[("tbA2",)] = (AE, AE.n)
            return sinb, cosb, AE.n

        q_xnT = DMA("sync", QS, xnT[:].rearrange("p (c n) -> p c n", c=8),
                    xnsT.rearrange("(c p) t -> p c t", p=128)[:, :, tb:tb + 1024],
                    waits=[(XS, max(xstore_q[tc_]
                                    for tc_ in range(sl * 16 + hf * 8,
                                                     sl * 16 + hf * 8 + 8))),
                           lastuse.get(("xnT",))])
        q_acc = None
        for i in range(8):
            q_acc = DMA("sync", QS, acc[:, i * 1024:(i + 1) * 1024],
                        x_c[tb + i * 128: tb + i * 128 + 128, :],
                        waits=[(QS, acc_store_q), lastuse.get(("acc",))])

        if debug and g == 0:
            V(lambda v: v.tensor_copy(fsc1[:, 0:64], rw[:]), waits=[(QW, qw_rw)], inc=VE)
            DMA("sync", QS, rw_d[:], fsc1[:, 0:64], waits=[(VE, VE.n)])
            DMA("sync", QS, xnt_d[:], xnT[:], waits=[(QS, q_xnT)])
            DMA("sync", QS, iob_d[:], iotabc[:], waits=[(GE, ge_iotabc)])
        ve_A = None
        def emit_router(i):
            nonlocal ve_A, prev_ae_ep
            bank = pb[2]
            for c in range(8):
                MM(bank[:, 0:8], xnT[:, c * 1024 + i * 128: c * 1024 + i * 128 + 128],
                   rw[:, c * 8:(c + 1) * 8], c == 0, c == 7,
                   waits=([(QS, q_xnT), (QW, qw_rw), bfree(bank)] if c == 0 else []),
                   inc=(PE if c == 7 else None))
            pe_r = PE.n
            V(lambda v, b=bank: v.tensor_copy(r8a[:], b[:, 0:8]),
              waits=[(PE, pe_r), (AE, prev_ae_ep)], inc=VE)
            bset(bank, (VE, VE.n))
            dbg_cap = debug and g == 0 and i == 0
            if dbg_cap:
                V(lambda v: v.tensor_copy(attsb[:, 0:8], r8a[:]))
            V(lambda v: v.reduce_max(cols[:, 6:7], r8a[:], AX, negate=True), inc=VE)
            ve_n = VE.n
            ACT(r8b[:], r8a[:], AF.Exp, bias=cols[:, 6:7], waits=[(VE, ve_n)], inc=AE)
            ae_z = AE.n
            if dbg_cap:
                V(lambda v: v.tensor_copy(attsb[:, 8:16], r8b[:]),
                  waits=[(AE, ae_z)])
            V(lambda v: v.reduce_sum(cols[:, 6:7], r8b[:], AX),
              waits=[(AE, ae_z)], inc=VE)
            ve_z = VE.n
            V(lambda v: v.reciprocal(cols[:, 7:8], cols[:, 6:7]),
              waits=[(VE, ve_z)], inc=VE)
            ve_rz = VE.n
            V(lambda v: v.tensor_scalar(r8a[:], r8b[:], cols[:, 7:8], None, ALU.mult),
              waits=[(VE, ve_rz)])
            if dbg_cap:
                V(lambda v: v.tensor_copy(attsb[:, 16:24], r8a[:]))
            V(lambda v: v.reduce_max(cols[:, 8:9], r8a[:], AX), inc=VE)
            ve_pm = VE.n
            V(lambda v: v.tensor_scalar(r8c[:], r8a[:], cols[:, 8:9], None, ALU.is_ge),
              waits=[(VE, ve_pm)])
            V(lambda v: v.scalar_tensor_tensor(r8b[:], r8c[:], -2.0, r8a[:],
                                               ALU.mult, ALU.add))
            V(lambda v: v.reduce_max(cols[:, 9:10], r8b[:], AX), inc=VE)
            ve_pm2 = VE.n
            V(lambda v: v.tensor_scalar(r8c[:], r8a[:], cols[:, 9:10], None, ALU.is_ge),
              waits=[(VE, ve_pm2)], inc=VE)
            ve_m2 = VE.n
            if dbg_cap:
                V(lambda v: v.tensor_copy(attsb[:, 24:32], r8c[:]),
                  waits=[(VE, ve_m2)])
            ACT(r8b[:], r8a[:], AF.Exp, waits=[(VE, ve_m2)], inc=AE)
            ae_ep = AE.n
            prev_ae_ep = ae_ep
            V(lambda v: v.tensor_mul(r8a[:], r8b[:], r8c[:]), waits=[(AE, ae_ep)])
            V(lambda v: v.reduce_sum(cols[:, 6:7], r8a[:], AX), inc=VE)
            ve_s2 = VE.n
            V(lambda v: v.reciprocal(cols[:, 10:11], cols[:, 6:7]),
              waits=[(VE, ve_s2)], inc=VE)
            ve_rs2 = VE.n
            V(lambda v: v.tensor_scalar(r8b[:], r8a[:], cols[:, 10:11], None, ALU.mult),
              waits=[(VE, ve_rs2)])
            if dbg_cap:
                V(lambda v: v.tensor_copy(attsb[:, 32:40], r8a[:]))
                V(lambda v: v.tensor_copy(attsb[:, 40:48], r8b[:]))
                V(lambda v: v.tensor_copy(attsb[:, 48:56], sbc[:]))
                V(lambda v: v.tensor_copy(attsb[:, 56:64], cols[:, 8:16][:, 0:8]),
                  inc=VE)
                DMA("sync", QS, rchain_d[:], attsb[:, 0:64], waits=[(VE, VE.n)])
            ve_A = V(lambda v, i=i: v.tensor_mul(Asb[:, i * 8:(i + 1) * 8],
                                                 r8b[:], sbc[:]),
                     waits=[(VE, ve_sbc)], inc=VE)

        for nt in range(2):
            tabnext = p2_tables(nt, 0)
            for m in range(8):
                sinb2, cosb2, ae_tab2 = tabnext
                bank = pb[m % 2]
                for c in range(8):
                    MM(bank[:, 0:512],
                       qub[:, c * 1024 + m * 128: c * 1024 + m * 128 + 128],
                       xnT[:, c * 1024 + nt * 512: c * 1024 + nt * 512 + 512],
                       c == 0, False,
                       waits=([(QW, qw_attnw[(g, "qu")]), (QS, q_xnT), bfree(bank)]
                              if c == 0 else []))
                MM(bank[:, 0:512], bq_row[0:1, m * 128: m * 128 + 128],
                   ones_bf[0:1, 0:512], False, True, inc=PE)
                pe_q = PE.n
                if nt == 0:
                    emit_router(m)
                qsl = m % 2
                ksl2 = ksb[:, qsl * 512:(qsl + 1) * 512]
                ACT(ksl2, bank[:, 0:512], AF.Silu,
                    waits=[(PE, pe_q), lastuse.get(("qs", qsl))], inc=AE)
                bset(bank, (AE, AE.n))
                ae_q = AE.n
                V(lambda v, cb=cosb2, ksl2=ksl2: v.tensor_mul(
                    fsc1[:, 0:512], ksl2, cb),
                  waits=[(AE, max(ae_tab2, ae_q))])
                V(lambda v, sb=sinb2, ksl2=ksl2: v.tensor_mul(
                    fsc2[:, 0:512], ksl2, sb))
                for hh in range(2):
                    h = 2 * m + hh
                    V(lambda v, h=h, hh=hh: v.tensor_copy(
                        big1[0:64, h * 512:(h + 1) * 512],
                        fsc1[hh * 64:hh * 64 + 64, 0:512]),
                      waits=([lastuse.get(("big1",))] if hh == 0 else []))
                    V(lambda v, h=h, hh=hh: v.tensor_copy(
                        big1[64:128, h * 512:(h + 1) * 512],
                        fsc2[hh * 64:hh * 64 + 64, 0:512]),
                      inc=(VE if hh == 1 else None))
                s2 = m % 2
                tabslot_use[("p2", s2)] = (VE, VE.n)
                lastuse[("qs", qsl)] = (VE, VE.n)
                if m + 1 < 8:
                    tabnext = p2_tables(nt, m + 1)
            ve_qrot = VE.n
            for tsub in range(4):
                for dhh in range(2):
                    bank = pb[dhh]
                    for c in range(8):
                        MM(bank[:, 0:512],
                           xnT[:, c * 1024 + nt * 512 + tsub * 128:
                               c * 1024 + nt * 512 + tsub * 128 + 128],
                           qub[:, 8192 + c * 1024 + dhh * 512:
                               8192 + c * 1024 + dhh * 512 + 512],
                           c == 0, False, waits=([bfree(bank)] if c == 0 else []))
                    MM(bank[:, 0:512], ones_bf[0:1, 0:128],
                       bq_row[0:1, 3 * D + dhh * 512: 3 * D + dhh * 512 + 512],
                       False, True, inc=PE)
                    pe_u = PE.n
                    ACT(usb[:, tsub * 1024 + dhh * 512:
                            tsub * 1024 + dhh * 512 + 512],
                        bank[:, 0:512], AF.Copy,
                        waits=[(PE, pe_u), lastuse.get(("usb", tsub))], inc=AE)
                    bset(bank, (AE, AE.n))
            ae_u = AE.n
            for i2 in range(4):
                i = nt * 4 + i2
                ab = (pb[5], pb[6])
                for h in range(NH):
                    bank = ab[h // 8]
                    w = []
                    if h == 0:
                        w = [(VE, ve_qrot), (AE, ae_kv), bfree(ab[0]), bfree(ab[1])]
                    MM(bank[:, (h % 8) * 64:(h % 8) * 64 + 64],
                       big1[:, h * 512 + i2 * 128: h * 512 + i2 * 128 + 128],
                       kvb[:, sl * 1024 + h * 64: sl * 1024 + h * 64 + 64],
                       True, True, waits=w,
                       inc=(PE if h == NH - 1 else None))
                pe_att = PE.n
                slot = i % 2
                st = stats[:, slot * 12:(slot + 1) * 12]
                mvs = mv[:, slot * 2:(slot + 1) * 2]
                V(lambda v, st=st: v.bn_stats(st[:, 0:6], pb[5][:, 0:512]),
                  waits=[(PE, pe_att)])
                V(lambda v, st=st: v.bn_stats(st[:, 6:12], pb[6][:, 0:512]), inc=VE)
                ve_bs2 = VE.n
                V(lambda v, st=st, mvs=mvs: v.bn_aggr(mvs, st),
                  waits=[(VE, ve_bs2)], inc=VE)
                ve_mv = VE.n
                rsd = cols[:, 2 + slot:3 + slot]
                rsd2 = cols[:, 4 + slot:5 + slot]
                ACT(rsd, mvs[:, 1:2], AF.Sqrt, bias=cols[:, 0:1],
                    waits=[(VE, ve_mv)], inc=AE)
                ae_sq = AE.n
                V(lambda v, rsd=rsd, rsd2=rsd2: v.reciprocal(rsd2, rsd),
                  waits=[(AE, ae_sq)], inc=VE)
                ve_rs = VE.n
                V(lambda v, mvs=mvs, rsd2=rsd2: v.tensor_scalar(
                    fsc1[:, 0:512], pb[5][:, 0:512], mvs[:, 0:1], rsd2,
                    ALU.subtract, ALU.mult),
                  waits=[(VE, ve_rs)])
                V(lambda v, mvs=mvs, rsd2=rsd2: v.tensor_scalar(
                    fsc1[:, 512:1024], pb[6][:, 0:512], mvs[:, 0:1], rsd2,
                    ALU.subtract, ALU.mult), inc=VE)
                bset(ab[0], (VE, VE.n))
                bset(ab[1], (VE, VE.n))
                V(lambda v: v.tensor_mul(fsc2[:], fsc1[:], lnwbc[:]),
                  waits=[(VE, ve_lnwbc), lastuse.get(("tbA2",))])
                V(lambda v: v.tensor_add(fsc1[:], fsc2[:], lnbbc[:]),
                  waits=[(VE, ve_lnbbc)])
                gsl = gbf[:, slot * 1024:(slot + 1) * 1024]
                V(lambda v, i2=i2, gsl=gsl: v.tensor_mul(
                    gsl, fsc1[:], usb[:, i2 * 1024:(i2 + 1) * 1024]),
                  waits=[(AE, ae_u), lastuse.get(("gbf", slot))], inc=VE)
                ve_g = VE.n
                lastuse[("usb", i2)] = (VE, ve_g)
                for c in range(8):
                    TRANS(ptb[:, c * 128:(c + 1) * 128],
                          gsl[:, c * 128:(c + 1) * 128],
                          waits=([(VE, ve_g), (QS, consts_q),
                                  lastuse.get(("ptb",))] if c == 0 else []),
                          inc=(PE if c == 7 else None))
                pe_tr = PE.n
                gts = gT[:, slot * 1024:(slot + 1) * 1024]
                V(lambda v, gts=gts: v.tensor_copy(gts, ptb[:]),
                  waits=[(PE, pe_tr), lastuse.get(("gT", slot))], inc=VE)
                ve_gt = VE.n
                lastuse[("ptb",)] = (VE, ve_gt)
                lastuse[("gbf", slot)] = (PE, pe_tr)
                for dhh in range(2):
                    bank = pb[dhh]
                    for c in range(8):
                        MM(bank[:, 0:512], gts[:, c * 128:(c + 1) * 128],
                           wob[:, c * 1024 + dhh * 512: c * 1024 + dhh * 512 + 512],
                           c == 0, False,
                           waits=([(VE, ve_gt), (QW, qw_attnw[(g, "wo")]),
                                   bfree(bank)] if c == 0 else []))
                    MM(bank[:, 0:512], ones_bf[0:1, 0:128],
                       bo_row[0:1, dhh * 512: dhh * 512 + 512], False, True, inc=PE)
                    pe_wo = PE.n
                    aslice = acc[:, i * 1024 + dhh * 512: i * 1024 + dhh * 512 + 512]
                    V(lambda v, b=bank, aslice=aslice: v.scalar_tensor_tensor(
                        aslice, b[:, 0:512], 1.0, aslice, ALU.mult, ALU.add),
                      waits=[(PE, pe_wo), (QS, q_acc)], inc=VE)
                    bset(bank, (VE, VE.n))
                    acc_last_ve[i * 2 + dhh] = VE.n
                lastuse[("gT", slot)] = (PE, PE.n)
            lastuse[("big1",)] = (PE, PE.n)
        if debug and g == 0:
            DMA("sync", QS, A_d[:], Asb[:], waits=[(VE, ve_A)])
            for i in range(8):
                DMA("sync", QS, att_d[:, i * 1024:(i + 1) * 1024],
                    acc[:, i * 1024:(i + 1) * 1024],
                    waits=[(VE, max(acc_last_ve[i * 2], acc_last_ve[i * 2 + 1]))])
            V(lambda v: v.tensor_copy(attsb[:], kvb[:]), waits=[(AE, ae_kv)], inc=VE)
            DMA("sync", QS, kv_d[:], attsb[:], waits=[(VE, VE.n)])
            V(lambda v: v.tensor_copy(attsb[:], big1[:, 0:2048]), inc=VE)
            DMA("sync", QS, q_d[:, 0:2048], attsb[:], waits=[(VE, VE.n)])
        pe_attn_done = PE.n
        pe_last_w1b_use = (PE, pe_attn_done)
        pe_last_w2b_use = (PE, pe_attn_done)
        load_w1(g, 0, 0)
        load_w2(g, 0, 0)

        for e_ in range(E):
            for h_ in range(2):
                for tt in range(2):
                    sil_ae = []
                    for fc in range(16):
                        bank = pb[3 + fc % 2]
                        for c in range(8):
                            MM(bank[:, 0:512],
                               w1b[:, c * 2048 + fc * 128: c * 2048 + fc * 128 + 128],
                               xnT[:, c * 1024 + tt * 512: c * 1024 + tt * 512 + 512],
                               c == 0, c == 7,
                               waits=([(QW, qw_w1[(g, e_, h_)]), (QS, q_xnT),
                                       bfree(bank)] if c == 0 else []),
                               inc=(PE if c == 7 else None))
                        pe_h = PE.n
                        ACT(big1[:, fc * 512:(fc + 1) * 512], bank[:, 0:512], AF.Silu,
                            waits=[(PE, pe_h), lastuse.get(("big1",))], inc=AE)
                        bset(bank, (AE, AE.n))
                        sil_ae.append(AE.n)
                    ae_hdone = AE.n
                    pe_H_last = PE.n
                    if tt == 1:
                        ne, nh = (e_, 1) if h_ == 0 else (e_ + 1, 0)
                        pe_last_w1b_use = (PE, pe_H_last)
                        if ne < E:
                            load_w1(g, ne, nh)
                        elif g < 3:
                            load_attn_qu(g + 1)
                        if ne >= E:
                            lastuse[("xnT",)] = (PE, pe_H_last)
                    for tsub in range(4):
                        i = tt * 4 + tsub
                        for dhh in range(2):
                            bank = pb[5 + dhh]
                            for fc in range(16):
                                w = [(AE, sil_ae[fc])]
                                if fc == 0:
                                    w += [(QW, qw_w2[(g, e_, h_)]), bfree(bank)]
                                MM(bank[:, 0:512],
                                   big1[:, fc * 512 + tsub * 128:
                                        fc * 512 + tsub * 128 + 128],
                                   w2b[:, fc * 1024 + dhh * 512:
                                       fc * 1024 + dhh * 512 + 512],
                                   fc == 0, fc == 15,
                                   waits=w,
                                   inc=(PE if fc == 15 else None))
                            pe_y = PE.n
                            aslice = acc[:, i * 1024 + dhh * 512:
                                         i * 1024 + dhh * 512 + 512]
                            acol = Asb[:, i * 8 + e_: i * 8 + e_ + 1]
                            V(lambda v, b=bank, aslice=aslice, acol=acol:
                              v.scalar_tensor_tensor(aslice, b[:, 0:512], acol,
                                                     aslice, ALU.mult, ALU.add),
                              waits=[(PE, pe_y), (VE, ve_A)], inc=VE)
                            bset(bank, (VE, VE.n))
                            acc_last_ve[i * 2 + dhh] = VE.n
                    lastuse[("big1",)] = (PE, PE.n)
                ne, nh = (e_, 1) if h_ == 0 else (e_ + 1, 0)
                pe_last_w2b_use = (PE, PE.n)
                if ne < E:
                    load_w2(g, ne, nh)
                elif g < 3:
                    load_attn_wo(g + 1)

        for i in range(8):
            acc_store_q = DMA(
                "sync", QS, out_c[tb + i * 128: tb + i * 128 + 128, :],
                acc[:, i * 1024:(i + 1) * 1024],
                waits=[(VE, max(acc_last_ve[i * 2], acc_last_ve[i * 2 + 1]))])

    _op("sync", lambda s_: s_.nop(), waits=[(QS, QS.n)])

    with nc.Block() as block:
        @block.sync
        def _(sync):
            for f in P["sync"]:
                f(sync)

        @block.scalar
        def _(scalar):
            for f in P["scalar"]:
                f(scalar)

        @block.vector
        def _(vector):
            for f in P["vector"]:
                f(vector)

        @block.tensor
        def _(tensor):
            for f in P["tensor"]:
                f(tensor)

        @block.gpsimd
        def _(gpsimd):
            for f in P["gpsimd"]:
                f(gpsimd)

    es.close()
    return nc


_CACHE = {}


def kernel(**inputs):
    _ = inputs.get("L0_attn_context")
    x = np.ascontiguousarray(np.asarray(inputs["L0_attn_x"], dtype=np.float32))
    wq = np.ascontiguousarray(np.asarray(inputs["L0_attn_Wqkvu"],
                                         dtype=np.float32).T)
    bq = np.ascontiguousarray(np.asarray(inputs["L0_attn_bqkvu"],
                                         dtype=np.float32)).reshape(1, -1)
    wo = np.ascontiguousarray(np.asarray(inputs["L0_attn_Wout"],
                                         dtype=np.float32).T)
    bo = np.ascontiguousarray(np.asarray(inputs["L0_attn_bout"],
                                         dtype=np.float32)).reshape(1, -1)
    theta = np.ascontiguousarray(np.asarray(inputs["L0_attn_theta"],
                                            dtype=np.float32)).reshape(-1)
    lnw = np.ascontiguousarray(np.asarray(inputs["L0_attn_ln_w"],
                                          dtype=np.float32)).reshape(1, -1)
    lnb = np.ascontiguousarray(np.asarray(inputs["L0_attn_ln_b"],
                                          dtype=np.float32)).reshape(1, -1)
    rwm = np.ascontiguousarray(np.asarray(inputs["L0_ffn_router_w"], dtype=np.float32))
    w1 = np.ascontiguousarray(np.asarray(inputs["L0_ffn_w1"],
                                         dtype=np.float32).transpose(0, 2, 1))
    w2 = np.ascontiguousarray(np.asarray(inputs["L0_ffn_w2"],
                                         dtype=np.float32).transpose(0, 2, 1))
    esc = np.ascontiguousarray(np.asarray(inputs["L0_ffn_expert_scale"],
                                          dtype=np.float32)).reshape(1, -1)

    th_row = np.ascontiguousarray(theta.reshape(1, -1))
    thcol = np.ascontiguousarray(theta.reshape(8, 128).T)
    rwT = np.ascontiguousarray(rwm.T)
    eye = np.eye(128, dtype=ml_dtypes.bfloat16)

    if "nc" not in _CACHE:
        _CACHE["nc"] = build_nc()
    nc = _CACHE["nc"]

    in_maps = []
    for c in range(NCORES):
        xc = np.ascontiguousarray(x[2 * c:2 * c + 2].reshape(T, D))
        in_maps.append({
            "x_c": xc, "wq_f": wq, "wout_f": wo, "rw_f": rwT,
            "w1_f": w1, "w2_f": w2, "bq_row": bq, "bo_row": bo,
            "th_row": th_row, "thcol": thcol, "lnw_row": lnw, "lnb_row": lnb,
            "sc_row": esc, "eye_d": eye,
        })
    res = run_bass_kernel_spmd(nc, in_maps, core_ids=list(range(NCORES)))
    out = np.empty((16, SEQ, D), dtype=np.float32)
    for c in range(NCORES):
        out[2 * c:2 * c + 2] = res.results[c]["out_c"].reshape(2, SEQ, D)
    return out


# version 29
# speedup vs baseline: 1.0006x; 1.0006x over previous
import numpy as np
import ml_dtypes
import concourse.bass as bass
import concourse.mybir as mybir
from concourse.bass_utils import run_bass_kernel_spmd
from contextlib import ExitStack

F32 = mybir.dt.float32
BF16 = mybir.dt.bfloat16
I32 = mybir.dt.int32
ALU = mybir.AluOpType
AF = mybir.ActivationFunctionType
AX = mybir.AxisListType.X

NCORES = 8
SEQ = 2048
D = 1024
NH = 16
DH = 64
E = 8
FD = 4096
T = 2 * SEQ
EPS = 1e-5
TWO_PI = float(2.0 * np.pi)
INV2PI = float(1.0 / (2.0 * np.pi))
HALF_PI = float(np.pi / 2.0)

ENGS = ("sync", "scalar", "vector", "tensor", "gpsimd")


class Sm:
    def __init__(self, h):
        self.h = h
        self.n = 0

    def bump(self, k=1):
        self.n += k
        return self.n


def build_nc(debug=False, audit=None):
    nc = bass.Bass("TRN2", target_bir_lowering=False)

    x_c = nc.dram_tensor("x_c", [T, D], F32, kind="ExternalInput")
    wq_f = nc.dram_tensor("wq_f", [D, 4 * D], F32, kind="ExternalInput")
    wout_f = nc.dram_tensor("wout_f", [D, D], F32, kind="ExternalInput")
    rw_f = nc.dram_tensor("rw_f", [D, E], F32, kind="ExternalInput")
    w1_f = nc.dram_tensor("w1_f", [E, D, FD], F32, kind="ExternalInput")
    w2_f = nc.dram_tensor("w2_f", [E, FD, D], F32, kind="ExternalInput")
    bq_row_d = nc.dram_tensor("bq_row", [1, 4 * D], F32, kind="ExternalInput")
    bo_row_d = nc.dram_tensor("bo_row", [1, D], F32, kind="ExternalInput")
    th_row_d = nc.dram_tensor("th_row", [1, D], F32, kind="ExternalInput")
    thcol_d = nc.dram_tensor("thcol", [128, 8], F32, kind="ExternalInput")
    lnw_row_d = nc.dram_tensor("lnw_row", [1, D], F32, kind="ExternalInput")
    lnb_row_d = nc.dram_tensor("lnb_row", [1, D], F32, kind="ExternalInput")
    sc_row_d = nc.dram_tensor("sc_row", [1, E], F32, kind="ExternalInput")
    eye_d = nc.dram_tensor("eye_d", [128, 128], BF16, kind="ExternalInput")
    out_c = nc.dram_tensor("out_c", [T, D], F32, kind="ExternalOutput")

    xnsT = nc.dram_tensor("xnsT", [D, T], BF16, kind="Internal")
    wqs = nc.dram_tensor("wqs", [D, 4 * D], BF16, kind="Internal")
    wouts = nc.dram_tensor("wouts", [D, D], BF16, kind="Internal")
    rws = nc.dram_tensor("rws", [D, E], BF16, kind="Internal")
    w1s = nc.dram_tensor("w1s", [E, D, FD], BF16, kind="Internal")
    w2s = nc.dram_tensor("w2s", [E, FD, D], BF16, kind="Internal")
    if debug:
        xns_d = nc.dram_tensor("xns_d", [D, T], BF16, kind="ExternalOutput")
        kv_d = nc.dram_tensor("kv_d", [128, 2048], F32, kind="ExternalOutput")
        A_d = nc.dram_tensor("A_d", [128, 64], F32, kind="ExternalOutput")
        att_d = nc.dram_tensor("att_d", [128, 8192], F32, kind="ExternalOutput")
        q_d = nc.dram_tensor("q_d", [128, 8192], F32, kind="ExternalOutput")
        xnt_d = nc.dram_tensor("xnt_d", [128, 8192], BF16, kind="ExternalOutput")
        iob_d = nc.dram_tensor("iob_d", [128, 1024], F32, kind="ExternalOutput")
        tab_d = nc.dram_tensor("tab_d", [128, 2048], F32, kind="ExternalOutput")
        qs_d = nc.dram_tensor("qs_d", [128, 512], F32, kind="ExternalOutput")
        tb1_d = nc.dram_tensor("tb1_d", [128, 2048], F32, kind="ExternalOutput")
        kr_d = nc.dram_tensor("kr_d", [128, 2048], F32, kind="ExternalOutput")
        kd_d = nc.dram_tensor("kd_d", [128, 1024], F32, kind="ExternalOutput")
        vd_d = nc.dram_tensor("vd_d", [128, 1024], F32, kind="ExternalOutput")
        lg_d = nc.dram_tensor("lg_d", [128, 8], F32, kind="ExternalOutput")
        rw_d = nc.dram_tensor("rw_d", [128, 64], F32, kind="ExternalOutput")
        rchain_d = nc.dram_tensor("rchain_d", [128, 64], F32, kind="ExternalOutput")

    es = ExitStack()
    A = es.enter_context

    acc = A(nc.sbuf_tensor("acc", [128, 8 * D], F32))
    xnT = A(nc.sbuf_tensor("xnT", [128, 8 * D], BF16))
    big1 = A(nc.sbuf_tensor("big1", [128, 16 * 512], BF16))
    w1b = A(nc.sbuf_tensor("w1b", [128, 8 * 2048], BF16))
    w2b = A(nc.sbuf_tensor("w2b", [128, 16 * 1024], BF16))
    kvb = A(nc.sbuf_tensor("kvb", [128, 2048], BF16))
    usb = A(nc.sbuf_tensor("usb", [128, 4096], BF16))
    gbf = A(nc.sbuf_tensor("gbf", [128, 2048], BF16))
    gT = A(nc.sbuf_tensor("gT", [128, 2048], BF16))
    eye = A(nc.sbuf_tensor("eye", [128, 128], BF16))
    rw = A(nc.sbuf_tensor("rw", [128, 64], BF16))

    attsb = A(nc.sbuf_tensor("attsb", [128, 2048], F32))
    fsc1 = A(nc.sbuf_tensor("fsc1", [128, 1024], F32))
    fsc2 = A(nc.sbuf_tensor("fsc2", [128, 1024], F32))
    ksb = A(nc.sbuf_tensor("ksb", [128, 1024], F32))
    tbarg = A(nc.sbuf_tensor("tbarg", [128, 1024], F32))
    tbsin = A(nc.sbuf_tensor("tbsin", [128, 1024], BF16))
    tbcos = A(nc.sbuf_tensor("tbcos", [128, 1024], BF16))
    shbc1 = A(nc.sbuf_tensor("shbc1", [128, 1024], F32))
    lnbbc = A(nc.sbuf_tensor("lnbbc", [128, 1024], F32))
    iotabc = A(nc.sbuf_tensor("iotabc", [128, 1024], F32))
    psc = A(nc.sbuf_tensor("psc", [128, 1024], F32))
    ki = A(nc.sbuf_tensor("ki", [128, 512], I32))
    stats = A(nc.sbuf_tensor("stats", [128, 24], F32))
    mv = A(nc.sbuf_tensor("mv", [128, 4], F32))
    cols = A(nc.sbuf_tensor("cols", [128, 16], F32))
    Asb = A(nc.sbuf_tensor("Asb", [128, 64], F32))
    r8a = A(nc.sbuf_tensor("r8a", [128, 8], F32))
    r8b = A(nc.sbuf_tensor("r8b", [128, 8], F32))
    r8c = A(nc.sbuf_tensor("r8c", [128, 8], F32))
    sbc = A(nc.sbuf_tensor("sbc", [128, 8], F32))
    thcol = A(nc.sbuf_tensor("thcol_s", [128, 8], F32))
    poscol = A(nc.sbuf_tensor("poscol", [128, 16], F32))
    bq_row = A(nc.sbuf_tensor("bq_row_s", [1, 4 * D], BF16))
    bo_row = A(nc.sbuf_tensor("bo_row_s", [1, D], BF16))
    ones_r = A(nc.sbuf_tensor("ones_r", [1, 128], F32))
    row_s = attsb[0:1, 0:1024]
    ones_bf = A(nc.sbuf_tensor("ones_bf", [1, 512], BF16))
    krot = usb[:, 0:2048]
    vsb = usb[:, 2048:3072]
    vsb2 = usb[:, 3072:4096]
    thetabc = shbc1
    lnwbc = shbc1

    pb = [A(nc.psum_tensor(f"pb{i}", [128, 512], F32)) for i in range(7)]
    ptb = A(nc.psum_tensor("ptb", [128, 1024], BF16))

    QS = Sm(A(nc.semaphore("qs")))
    QW = Sm(A(nc.semaphore("qw")))
    SC = Sm(A(nc.semaphore("sc")))
    KL = Sm(A(nc.semaphore("kl")))
    XS = Sm(A(nc.semaphore("xs")))
    PE = Sm(A(nc.semaphore("pe")))
    VE = Sm(A(nc.semaphore("ve")))
    AE = Sm(A(nc.semaphore("ae")))
    GE = Sm(A(nc.semaphore("ge")))

    P = {e: [] for e in ENGS}
    WAITED = {e: {} for e in ENGS}

    def _op(eng, build, waits=(), inc=None, inck=1):
        wl = []
        for w in waits:
            if w is None:
                continue
            sem, tgt = w
            if tgt is None or tgt <= 0:
                continue
            prev = WAITED[eng].get(id(sem), 0)
            if tgt > prev:
                WAITED[eng][id(sem)] = tgt
                wl.append((sem.h, tgt))
        ih = None
        if inc is not None:
            inc.bump(inck)
            ih, iv = inc.h, inck
        if audit is not None:
            audit.append((eng, [(id(s_), t_) for s_, t_ in
                                [(w[0], w[1]) for w in waits
                                 if w is not None and w[1] and w[1] > 0]],
                          (id(inc), inck) if inc is not None else None,
                          len(audit)))

        def f(eng_obj):
            for h, t in wl:
                eng_obj.wait_ge(h, t)
            ins = build(eng_obj)
            if ih is not None:
                ins.then_inc(ih, iv)
        P[eng].append(f)
        return inc.n if inc is not None else None

    def DMA(eng, qsem, out_ap, in_ap, waits=(), transpose=False, **kw):
        _op(eng, lambda o: o.dma_start(out_ap, in_ap, transpose=transpose, **kw),
            waits, inc=qsem, inck=16)
        return qsem.n

    def MM(out, l, r, start, stop, waits=(), inc=None, skip=False):
        return _op("tensor",
                   lambda t: t.matmul(out, l, r, start=start, stop=stop,
                                      skip_group_check=skip),
                   waits, inc)

    def TRANS(out, in_, waits=(), inc=None):
        return _op("tensor", lambda t: t.transpose(out, in_, eye[:]), waits, inc)

    def ACT(out, in_, func, bias=None, scale=None, accum=None, waits=(), inc=None):
        def b(s_):
            kw = {}
            if bias is not None:
                kw["bias"] = bias
            if scale is not None:
                kw["scale"] = scale
            if accum is not None:
                kw["accum_out"] = accum
            return s_.activation(out, in_, func, **kw)
        return _op("scalar", b, waits, inc)

    def V(fn, waits=(), inc=None):
        w = list(waits) + [(VE, VE.n)]
        return _op("vector", fn, w, inc=VE)

    def G(fn, waits=(), inc=None):
        w = list(waits) + [(GE, GE.n)]
        return _op("gpsimd", fn, w, inc=GE)

    bankfree = {}

    def bfree(bank):
        return bankfree.get(id(bank))

    def bset(bank, semcount):
        bankfree[id(bank)] = semcount

    lastuse = {}

    DMA("sync", QS, eye[:], eye_d[:])
    DMA("sync", QS, thcol[:], thcol_d[:])
    DMA("sync", QS, row_s, th_row_d[:])
    q_sc = DMA("sync", QS, lnbbc[0:1, 0:E], sc_row_d[:])
    q_throw = q_sc
    consts_q = QS.n
    DMA("gpsimd", SC, bq_row[:], bq_row_d[:])
    DMA("gpsimd", SC, bo_row[:], bo_row_d[:])
    sc_bias = SC.n

    for pc in range(16):
        G(lambda g, pc=pc: g.iota(poscol[:, pc:pc + 1], [[1, 1]], base=pc * 128,
                                  channel_multiplier=1,
                                  allow_small_or_imprecise_dtypes=True))
    G(lambda g: g.iota(iotabc[:], [[1, 1024]], base=0,
                       channel_multiplier=0,
                       allow_small_or_imprecise_dtypes=True))
    G(lambda g: g.engine_nop(), inc=GE)
    ge_iota = GE.n

    def _gnop():
        _op("gpsimd", lambda g_: g_.engine_nop())

    DMA("gpsimd", SC, rws[:], rw_f[:], max_dma_last_dim=8192)
    sc_router = SC.n
    _gnop()
    DMA("gpsimd", SC, wqs[:, 0:D], wq_f[:, 0:D], max_dma_last_dim=8192)
    DMA("gpsimd", SC, wqs[:, 3 * D:4 * D], wq_f[:, 3 * D:4 * D],
        max_dma_last_dim=8192)
    sc_qkvu = SC.n
    _gnop()
    DMA("gpsimd", SC, wouts[:], wout_f[:], max_dma_last_dim=8192)
    sc_wout = SC.n
    _gnop()
    sc_w1 = {}
    sc_w2 = {}
    for e_ in range(E):
        for h_ in range(2):
            DMA("gpsimd", SC, w1s[e_, :, h_ * 2048:(h_ + 1) * 2048],
                w1_f[e_, :, h_ * 2048:(h_ + 1) * 2048], max_dma_last_dim=8192)
            sc_w1[(e_, h_)] = SC.n
            _gnop()
            DMA("gpsimd", SC, w2s[e_, h_ * 2048:(h_ + 1) * 2048, :],
                w2_f[e_, h_ * 2048:(h_ + 1) * 2048, :], max_dma_last_dim=8192)
            sc_w2[(e_, h_)] = SC.n
            _gnop()

    V(lambda v: v.memset(ones_r[:], 1.0))
    V(lambda v: v.memset(ones_bf[:], 1.0))
    V(lambda v: v.memset(cols[:, 0:1], EPS))
    V(lambda v: v.memset(cols[:, 1:2], HALF_PI))
    V(lambda v: v.memset(cols[:, 11:12], -1.0), inc=VE)
    ve_memset = VE.n

    def bcast_row(dst_sbuf, src_row_ap, ncols, waits_pe):
        last = None
        for j in range(0, ncols, 512):
            bank = pb[2 + (j // 512) % 2]
            MM(bank[:, 0:512], ones_r[0:1, 0:128], src_row_ap[0:1, j:j + 512],
               True, True, waits=list(waits_pe) + [bfree(bank)], inc=PE)
            pe_n = PE.n
            V(lambda v, b=bank, j=j: v.tensor_copy(dst_sbuf[:, j:j + 512], b[:, 0:512]),
              waits=[(PE, pe_n)], inc=VE)
            bset(bank, (VE, VE.n))
            last = VE.n
        return last

    ve_thetabc = bcast_row(thetabc, row_s, D, [(QS, q_throw), (VE, ve_memset)])
    bank = pb[2]
    MM(bank[:, 0:8], ones_r[0:1, 0:128], lnbbc[0:1, 0:E], True, True,
       waits=[(QS, q_sc), bfree(bank)], inc=PE)
    pe_sbc = PE.n
    V(lambda v, b=bank: v.tensor_copy(sbc[:], b[:, 0:8]), waits=[(PE, pe_sbc)], inc=VE)
    bset(bank, (VE, VE.n))
    ve_sbc = VE.n
    lastuse[("xa", 0)] = (PE, PE.n)

    DMA("scalar", KL, acc[:].rearrange("p (c n) -> p c n", c=8),
        wq_f[:, D:2 * D].rearrange("(c p) n -> p c n", p=128))
    kl_k = KL.n
    ACT(w1b[:, 0:8192], acc[:], AF.Copy, waits=[(KL, kl_k)], inc=AE)
    ae_kvk = AE.n
    DMA("scalar", KL, acc[:].rearrange("p (c n) -> p c n", c=8),
        wq_f[:, 2 * D:3 * D].rearrange("(c p) n -> p c n", p=128),
        waits=[(AE, ae_kvk)])
    kl_v = KL.n
    ACT(w1b[:, 8192:16384], acc[:], AF.Copy, waits=[(KL, kl_v)], inc=AE)
    ae_kvv = AE.n
    lastuse[("acc",)] = (AE, ae_kvv)

    qw_rw = DMA("scalar", QW, rw[:].rearrange("p (c e) -> p c e", e=8),
                rws[:].rearrange("(c p) e -> p c e", p=128), waits=[(SC, sc_router)])

    xstore_q = [0] * 32
    kv_first = [True] * 4
    tabslot_use = {}

    def p1_tables(pc):
        s = pc % 2
        sinb = tbsin[:] if s == 0 else big1[:, 0:1024]
        cosb = tbcos[:] if s == 0 else big1[:, 1024:2048]
        V(lambda v, pc=pc: v.tensor_scalar(tbarg[:], thetabc[:], poscol[:, pc:pc + 1],
                                           None, ALU.mult),
          waits=[(VE, ve_thetabc), (GE, ge_iota), lastuse.get(("tbA",))])
        V(lambda v: v.tensor_scalar(fsc1[:, 0:512], tbarg[:, 0:512], INV2PI,
                                    None, ALU.mult))
        V(lambda v: v.tensor_copy(ki[:], fsc1[:, 0:512]))
        V(lambda v: v.tensor_copy(fsc1[:, 0:512], ki[:]))
        V(lambda v: v.scalar_tensor_tensor(fsc2[:, 0:512], fsc1[:, 0:512], -TWO_PI,
                                           tbarg[:, 0:512], ALU.mult, ALU.add))
        V(lambda v: v.tensor_scalar(fsc1[:, 0:512], tbarg[:, 0:512], INV2PI, 0.25,
                                    ALU.mult, ALU.add))
        V(lambda v: v.tensor_copy(ki[:], fsc1[:, 0:512]))
        V(lambda v: v.tensor_copy(fsc1[:, 0:512], ki[:]))
        V(lambda v: v.scalar_tensor_tensor(fsc2[:, 512:1024], fsc1[:, 0:512], -TWO_PI,
                                           tbarg[:, 0:512], ALU.mult, ALU.add),
          inc=VE)
        ve_args = VE.n
        sw = tabslot_use.get((s, "sin"))
        cw = tabslot_use.get((s, "cos"))
        ACT(sinb[:, 0:512], fsc2[:, 0:512], AF.Sin, waits=[(VE, ve_args), sw, cw])
        ACT(cosb[:, 0:512], fsc2[:, 512:1024], AF.Sin, bias=cols[:, 1:2])
        ACT(sinb[:, 512:1024], tbarg[:, 512:1024], AF.Sin)
        ACT(cosb[:, 512:1024], tbarg[:, 512:1024], AF.Sin,
            bias=cols[:, 1:2], inc=AE)
        lastuse[("tbA",)] = (AE, AE.n)
        return sinb, cosb, AE.n

    tiles = [(pc, sl) for pc in range(16) for sl in range(2)]
    q_x = [0] * 32

    def p1_load(i):
        pc, sl = tiles[i]
        tr = sl * SEQ + pc * 128
        slot = i % 2
        xa = attsb[:, slot * 1024:(slot + 1) * 1024]
        q_x[i] = DMA("sync", QS, xa, x_c[tr:tr + 128, :],
                     waits=[lastuse.get(("xa", slot))])
        _op("sync", lambda s_: s_.nop())

    def p1_ln(i, tab):
        pc, sl = tiles[i]
        tc = sl * 16 + pc
        tr = sl * SEQ + pc * 128
        slot = i % 2
        xa = attsb[:, slot * 1024:(slot + 1) * 1024]
        qx = q_x[i]
        st = stats[:, slot * 12:(slot + 1) * 12]
        mvs = mv[:, slot * 2:(slot + 1) * 2]
        V(lambda v, xa=xa, st=st: v.bn_stats(st[:, 0:6], xa[:, 0:512]),
          waits=[(QS, qx)])
        V(lambda v, xa=xa, st=st: v.bn_stats(st[:, 6:12], xa[:, 512:1024]), inc=VE)
        ve_bs = VE.n
        V(lambda v, st=st, mvs=mvs: v.bn_aggr(mvs, st), waits=[(VE, ve_bs)], inc=VE)
        ve_mv = VE.n
        rsd = cols[:, 2 + slot:3 + slot]
        rsd2 = cols[:, 4 + slot:5 + slot]
        ACT(rsd, mvs[:, 1:2], AF.Sqrt, bias=cols[:, 0:1], waits=[(VE, ve_mv)], inc=AE)
        ae_s = AE.n
        V(lambda v, rsd=rsd, rsd2=rsd2: v.reciprocal(rsd2, rsd),
          waits=[(AE, ae_s)], inc=VE)
        ve_r = VE.n
        xw = gbf[:, slot * 1024:(slot + 1) * 1024]
        V(lambda v, xa=xa, mvs=mvs, rsd2=rsd2, xw=xw:
          v.tensor_scalar(xw, xa, mvs[:, 0:1], rsd2, ALU.subtract, ALU.mult),
          waits=[(VE, ve_r), lastuse.get(("gbf", slot))], inc=VE)
        ve_xn = VE.n
        lastuse[("xa", slot)] = (VE, ve_xn)
        lastuse[("attsb", slot)] = (VE, ve_xn)
        return dict(pc=pc, sl=sl, slot=slot, tc=tc, tr=tr, ve_xn=ve_xn, xw=xw,
                    tab=tab)

    def p1_rest(stt_):
        pc, sl, slot = stt_["pc"], stt_["sl"], stt_["slot"]
        tc, tr = stt_["tc"], stt_["tr"]
        xw, ve_xn = stt_["xw"], stt_["ve_xn"]
        sinb, cosb, ae_tab1 = stt_["tab"]
        s = pc % 2
        xp = gT[:, slot * 1024:(slot + 1) * 1024]
        for c in range(8):
            TRANS(ptb[:, c * 128:(c + 1) * 128], xw[:, c * 128:(c + 1) * 128],
                  waits=([(VE, ve_xn), (QS, consts_q), lastuse.get(("ptb",))]
                         if c == 0 else []),
                  inc=(PE if c == 7 else None))
        pe_tp = PE.n
        lastuse[("gbf", slot)] = (PE, pe_tp)
        ACT(xp, ptb[:], AF.Copy, waits=[(PE, pe_tp), lastuse.get(("gTa", slot)),
                                        lastuse.get(("gTq", slot))],
            inc=AE)
        ae_xp = AE.n
        lastuse[("ptb",)] = (AE, ae_xp)
        qst = DMA("scalar", XS,
                  xnsT.rearrange("(c p) t -> p c t", p=128)[:, :, tr:tr + 128],
                  xp.rearrange("p (c t) -> p c t", t=128),
                  waits=[(AE, ae_xp)])
        xstore_q[tc] = qst
        lastuse[("gTq", slot)] = (XS, qst)
        ksl = ksb if slot == 0 else psc
        vsl = vsb if slot == 0 else vsb2
        pe_k = [0, 0]
        for half in range(2):
            bank = pb[half]
            for c in range(8):
                MM(bank[:, 0:512], xp[:, c * 128:(c + 1) * 128],
                   w1b[:, c * 1024 + half * 512: c * 1024 + half * 512 + 512],
                   c == 0, False,
                   waits=([(AE, ae_xp), (AE, ae_kvk), bfree(bank)]
                          if c == 0 else []))
            MM(bank[:, 0:512], ones_bf[0:1, 0:128],
               bq_row[0:1, D + half * 512: D + half * 512 + 512],
               False, True, waits=[(QS, consts_q), (SC, sc_bias)], inc=PE)
            pe_k[half] = PE.n
        for half in range(2):
            bank = pb[half]
            ACT(ksl[:, half * 512:(half + 1) * 512], bank[:, 0:512], AF.Silu,
                waits=[(PE, pe_k[half]), lastuse.get(("ksb", slot))], inc=AE)
            bset(bank, (AE, AE.n))
        ae_k = AE.n
        V(lambda v, cosb=cosb, ksl=ksl: v.tensor_mul(fsc1[:], ksl[:], cosb),
          waits=[(AE, max(ae_k, ae_tab1)), lastuse.get(("krot",)),
                 lastuse.get(("tbA",))])
        V(lambda v, sinb=sinb, ksl=ksl: v.tensor_mul(fsc2[:], ksl[:], sinb))
        kr4 = krot[:].rearrange("p (h two dh) -> p h two dh", two=2, dh=DH)
        V(lambda v, kr4=kr4: v.tensor_copy(
            kr4[:, :, 0, :], fsc1[:].rearrange("p (h dh) -> p h dh", dh=DH)))
        V(lambda v, kr4=kr4: v.tensor_copy(
            kr4[:, :, 1, :], fsc2[:].rearrange("p (h dh) -> p h dh", dh=DH)),
          inc=VE)
        ve_krot = VE.n
        lastuse[("ksb", slot)] = (VE, ve_krot)
        tabslot_use[(s, "sin")] = (VE, ve_krot)
        tabslot_use[(s, "cos")] = (VE, ve_krot)
        pe_v = [0, 0]
        for half in range(2):
            bank = pb[half]
            for c in range(8):
                MM(bank[:, 0:512], xp[:, c * 128:(c + 1) * 128],
                   w1b[:, 8192 + c * 1024 + half * 512:
                       8192 + c * 1024 + half * 512 + 512],
                   c == 0, False,
                   waits=([(AE, ae_kvv), bfree(bank)] if c == 0 else []))
            MM(bank[:, 0:512], ones_bf[0:1, 0:128],
               bq_row[0:1, 2 * D + half * 512: 2 * D + half * 512 + 512],
               False, True, inc=PE)
            pe_v[half] = PE.n
        for half in range(2):
            bank = pb[half]
            ACT(vsl[:, half * 512:(half + 1) * 512], bank[:, 0:512], AF.Copy,
                waits=[(PE, pe_v[half]), lastuse.get(("vsb", slot))], inc=AE)
            bset(bank, (AE, AE.n))
        ae_v = AE.n
        for h in range(NH):
            kvbank = pb[3 + sl * 2 + h // 8]
            bi = sl * 2 + h // 8
            first = kv_first[bi]
            kv_first[bi] = False
            w = []
            if h == 0:
                w = [(VE, ve_krot), (AE, ae_v)]
                if first:
                    w += [bfree(pb[3 + sl * 2]), bfree(pb[4 + sl * 2])]
            MM(kvbank[:, (h % 8) * 64:(h % 8) * 64 + 64],
               krot[:, h * 128:(h + 1) * 128], vsl[:, h * 64:(h + 1) * 64],
               first, pc == 15, waits=w, skip=True,
               inc=(PE if h == NH - 1 else None))
        lastuse[("krot",)] = (PE, PE.n)
        lastuse[("vsb", slot)] = (PE, PE.n)
        lastuse[("gTa", slot)] = (PE, PE.n)

    p1_load(0)
    p1_load(1)
    tabs = {0: p1_tables(0)}
    lns = [None] * 32
    lns[0] = p1_ln(0, tabs[0])
    qw_qu0 = None
    for i in range(32):
        if i + 2 < 32:
            p1_load(i + 2)
        if i + 1 < 32:
            if (i + 1) % 2 == 0:
                tabs[(i + 1) // 2] = p1_tables((i + 1) // 2)
            lns[i + 1] = p1_ln(i + 1, tabs[(i + 1) // 2])
        p1_rest(lns[i])
    pe_kv_done = PE.n
    lastuse[("gT", 0)] = (XS, xstore_q[15])
    lastuse[("gT", 1)] = (XS, xstore_q[31])
    for sl in range(2):
        for bi in range(2):
            bank = pb[3 + sl * 2 + bi]
            ACT(kvb[:, sl * 1024 + bi * 512: sl * 1024 + bi * 512 + 512],
                bank[:, 0:512], AF.Copy, waits=[(PE, pe_kv_done)], inc=AE)
            bset(bank, (AE, AE.n))
    ae_kv = AE.n
    lastuse[("qs", 0)] = (VE, VE.n)
    lastuse[("qs", 1)] = (VE, VE.n)
    lastuse[("big1",)] = (AE, AE.n)
    for _ts in range(4):
        lastuse[("usb", _ts)] = (PE, pe_kv_done)
    q_lnw = DMA("sync", QS, row_s, lnw_row_d[:], waits=[(PE, PE.n), (VE, VE.n)])
    ve_lnwbc = bcast_row(lnwbc, row_s, D, [(QS, q_lnw), (VE, VE.n), (AE, ae_kv)])
    q_lnb = DMA("sync", QS, row_s, lnb_row_d[:], waits=[(PE, PE.n)])
    ve_lnbbc = bcast_row(lnbbc, row_s, D, [(QS, q_lnb)])
    lastuse[("attsb", 0)] = (PE, PE.n)
    if debug:
        DMA("sync", QS, xns_d[:], xnsT[:], waits=[(XS, max(xstore_q))])
        V(lambda v: v.tensor_copy(attsb[:], big1[:, 0:2048]),
          waits=[(PE, pe_kv_done)], inc=VE)
        q_tb1 = DMA("sync", QS, tb1_d[:], attsb[:], waits=[(VE, VE.n)])
        V(lambda v: v.tensor_copy(attsb[:], krot[:]),
          waits=[(QS, q_tb1)], inc=VE)
        q_tb1 = DMA("sync", QS, kr_d[:], attsb[:], waits=[(VE, VE.n)])
        V(lambda v: v.tensor_copy(attsb[:, 0:1024], psc[:]),
          waits=[(QS, q_tb1)])
        V(lambda v: v.tensor_copy(attsb[:, 1024:2048], vsb2[:]), inc=VE)
        q_tb1 = DMA("sync", QS, kd_d[:], attsb[:, 0:1024], waits=[(VE, VE.n)])
        q_tb1 = DMA("sync", QS, vd_d[:], attsb[:, 1024:2048])
        lastuse[("attsb", 0)] = (QS, q_tb1)
        lastuse[("attsb", 1)] = (QS, q_tb1)

    pe_last_w1b_use = (PE, pe_kv_done)
    pe_last_w2b_use = None
    qw_attnw = {}
    qw_w1 = {}
    qw_w2 = {}
    acc_store_q = 0
    acc_last_ve = [0] * 16
    prev_ae_ep = 0

    def load_attn_qu(g):
        DMA("scalar", QW, w1b[:, 0:8192].rearrange("p (c n) -> p c n", c=8),
            wqs[:, 0:D].rearrange("(c p) n -> p c n", p=128),
            waits=[pe_last_w1b_use, (SC, sc_qkvu)])
        last = DMA("scalar", QW,
                   w1b[:, 8192:16384].rearrange("p (c n) -> p c n", c=8),
                   wqs[:, 3 * D:4 * D].rearrange("(c p) n -> p c n", p=128))
        qw_attnw[(g, "qu")] = last
        _op("scalar", lambda s_: s_.nop())

    def load_attn_wo(g):
        last = DMA("scalar", QW,
                   w2b[:, 0:8192].rearrange("p (c n) -> p c n", c=8),
                   wouts[:].rearrange("(c p) n -> p c n", p=128),
                   waits=[pe_last_w2b_use, (SC, sc_wout)])
        qw_attnw[(g, "wo")] = last
        _op("scalar", lambda s_: s_.nop())

    def load_w1(g, e_, h_):
        last = DMA("scalar", QW, w1b[:].rearrange("p (c n) -> p c n", c=8),
                   w1s[e_, :, h_ * 2048:(h_ + 1) * 2048]
                   .rearrange("(c p) n -> p c n", p=128),
                   waits=[pe_last_w1b_use, (SC, sc_w1[(e_, h_)])])
        qw_w1[(g, e_, h_)] = last
        _op("scalar", lambda s_: s_.nop())

    def load_w2(g, e_, h_):
        last = DMA("scalar", QW, w2b[:].rearrange("p (fc n) -> p fc n", fc=16),
                   w2s[e_, h_ * 2048:(h_ + 1) * 2048, :]
                   .rearrange("(fc p) n -> p fc n", p=128),
                   waits=[pe_last_w2b_use, (SC, sc_w2[(e_, h_)])])
        qw_w2[(g, e_, h_)] = last
        _op("scalar", lambda s_: s_.nop())

    load_attn_qu(0)
    load_attn_wo(0)

    for g in range(4):
        sl, hf = g // 2, g % 2
        tb = sl * SEQ + hf * 1024
        qub = w1b
        wob = w2b
        iot = iotabc
        ge_iotabc = ge_iota
        ve_iot = None
        if g > 0:
            shift = 1024.0 if hf == 1 else -1024.0
            V(lambda v, shift=shift: v.tensor_scalar(
                iotabc[:], iotabc[:], shift, None, ALU.add),
              waits=[(GE, ge_iota), lastuse.get(("iotabc",))], inc=VE)
            ve_iot = VE.n

        def p2_tables(nt, m):
            s2 = m % 2
            sinb = tbsin[:, s2 * 512:(s2 + 1) * 512]
            cosb = tbcos[:, s2 * 512:(s2 + 1) * 512]
            sw = tabslot_use.get(("p2", s2))
            if m >= 3:
                V(lambda v, nt=nt, m=m: v.tensor_scalar(
                    fsc2[:, 0:512], iot[:, nt * 512: nt * 512 + 512],
                    thcol[:, m:m + 1], None, ALU.mult),
                  waits=[(GE, ge_iotabc), (VE, ve_iot) if ve_iot else None,
                         (QS, consts_q), lastuse.get(("tbA2",))])
                V(lambda v: v.tensor_scalar(fsc2[:, 512:1024], fsc2[:, 0:512],
                                            -1.0, None, ALU.mult), inc=VE)
                ve_a = VE.n
                ACT(sinb, fsc2[:, 0:512], AF.Sin, waits=[(VE, ve_a), sw])
                ACT(cosb, fsc2[:, 512:1024], AF.Sin, bias=cols[:, 1:2], inc=AE)
                lastuse[("tbA2",)] = (AE, AE.n)
                return sinb, cosb, AE.n
            V(lambda v, nt=nt, m=m: v.tensor_scalar(
                tbarg[:, 0:512], iot[:, nt * 512: nt * 512 + 512],
                thcol[:, m:m + 1], None, ALU.mult),
              waits=[(GE, ge_iotabc), (VE, ve_iot) if ve_iot else None,
                     (QS, consts_q), lastuse.get(("tbA2",))])
            V(lambda v: v.tensor_scalar(fsc1[:, 0:512], tbarg[:, 0:512], INV2PI,
                                        None, ALU.mult))
            V(lambda v: v.tensor_copy(ki[:], fsc1[:, 0:512]))
            V(lambda v: v.tensor_copy(fsc1[:, 0:512], ki[:]))
            V(lambda v: v.scalar_tensor_tensor(fsc2[:, 0:512], fsc1[:, 0:512],
                                               -TWO_PI, tbarg[:, 0:512],
                                               ALU.mult, ALU.add))
            V(lambda v: v.tensor_scalar(fsc1[:, 0:512], tbarg[:, 0:512], INV2PI,
                                        0.25, ALU.mult, ALU.add))
            V(lambda v: v.tensor_copy(ki[:], fsc1[:, 0:512]))
            V(lambda v: v.tensor_copy(fsc1[:, 0:512], ki[:]))
            V(lambda v: v.scalar_tensor_tensor(fsc2[:, 512:1024], fsc1[:, 0:512],
                                               -TWO_PI, tbarg[:, 0:512],
                                               ALU.mult, ALU.add), inc=VE)
            ve_args = VE.n
            ACT(sinb, fsc2[:, 0:512], AF.Sin, waits=[(VE, ve_args), sw])
            ACT(cosb, fsc2[:, 512:1024], AF.Sin, bias=cols[:, 1:2], inc=AE)
            lastuse[("tbA2",)] = (AE, AE.n)
            return sinb, cosb, AE.n

        q_xnT = DMA("sync", QS, xnT[:].rearrange("p (c n) -> p c n", c=8),
                    xnsT.rearrange("(c p) t -> p c t", p=128)[:, :, tb:tb + 1024],
                    waits=[(XS, max(xstore_q[tc_]
                                    for tc_ in range(sl * 16 + hf * 8,
                                                     sl * 16 + hf * 8 + 8))),
                           lastuse.get(("xnT",))])
        _op("sync", lambda s_: s_.nop())
        q_acc = None
        for i in range(8):
            q_acc = DMA("sync", QS, acc[:, i * 1024:(i + 1) * 1024],
                        x_c[tb + i * 128: tb + i * 128 + 128, :],
                        waits=[(QS, acc_store_q), lastuse.get(("acc",))])

        if debug and g == 0:
            V(lambda v: v.tensor_copy(fsc1[:, 0:64], rw[:]), waits=[(QW, qw_rw)], inc=VE)
            DMA("sync", QS, rw_d[:], fsc1[:, 0:64], waits=[(VE, VE.n)])
            DMA("sync", QS, xnt_d[:], xnT[:], waits=[(QS, q_xnT)])
            DMA("sync", QS, iob_d[:], iotabc[:], waits=[(GE, ge_iotabc)])
        ve_A = None
        def emit_router(i):
            nonlocal ve_A, prev_ae_ep
            bank = pb[2]
            for c in range(8):
                MM(bank[:, 0:8], xnT[:, c * 1024 + i * 128: c * 1024 + i * 128 + 128],
                   rw[:, c * 8:(c + 1) * 8], c == 0, c == 7,
                   waits=([(QS, q_xnT), (QW, qw_rw), bfree(bank)] if c == 0 else []),
                   inc=(PE if c == 7 else None))
            pe_r = PE.n
            V(lambda v, b=bank: v.tensor_copy(r8a[:], b[:, 0:8]),
              waits=[(PE, pe_r), (AE, prev_ae_ep)], inc=VE)
            bset(bank, (VE, VE.n))
            dbg_cap = debug and g == 0 and i == 0
            if dbg_cap:
                V(lambda v: v.tensor_copy(attsb[:, 0:8], r8a[:]))
            V(lambda v: v.reduce_max(cols[:, 6:7], r8a[:], AX, negate=True), inc=VE)
            ve_n = VE.n
            ACT(r8b[:], r8a[:], AF.Exp, bias=cols[:, 6:7], waits=[(VE, ve_n)], inc=AE)
            ae_z = AE.n
            if dbg_cap:
                V(lambda v: v.tensor_copy(attsb[:, 8:16], r8b[:]),
                  waits=[(AE, ae_z)])
            V(lambda v: v.reduce_sum(cols[:, 6:7], r8b[:], AX),
              waits=[(AE, ae_z)], inc=VE)
            ve_z = VE.n
            V(lambda v: v.reciprocal(cols[:, 7:8], cols[:, 6:7]),
              waits=[(VE, ve_z)], inc=VE)
            ve_rz = VE.n
            V(lambda v: v.tensor_scalar(r8a[:], r8b[:], cols[:, 7:8], None, ALU.mult),
              waits=[(VE, ve_rz)])
            if dbg_cap:
                V(lambda v: v.tensor_copy(attsb[:, 16:24], r8a[:]))
            V(lambda v: v.reduce_max(cols[:, 8:9], r8a[:], AX), inc=VE)
            ve_pm = VE.n
            V(lambda v: v.tensor_scalar(r8c[:], r8a[:], cols[:, 8:9], None, ALU.is_ge),
              waits=[(VE, ve_pm)])
            V(lambda v: v.scalar_tensor_tensor(r8b[:], r8c[:], -2.0, r8a[:],
                                               ALU.mult, ALU.add))
            V(lambda v: v.reduce_max(cols[:, 9:10], r8b[:], AX), inc=VE)
            ve_pm2 = VE.n
            V(lambda v: v.tensor_scalar(r8c[:], r8a[:], cols[:, 9:10], None, ALU.is_ge),
              waits=[(VE, ve_pm2)], inc=VE)
            ve_m2 = VE.n
            if dbg_cap:
                V(lambda v: v.tensor_copy(attsb[:, 24:32], r8c[:]),
                  waits=[(VE, ve_m2)])
            ACT(r8b[:], r8a[:], AF.Exp, waits=[(VE, ve_m2)], inc=AE)
            ae_ep = AE.n
            prev_ae_ep = ae_ep
            V(lambda v: v.tensor_mul(r8a[:], r8b[:], r8c[:]), waits=[(AE, ae_ep)])
            V(lambda v: v.reduce_sum(cols[:, 6:7], r8a[:], AX), inc=VE)
            ve_s2 = VE.n
            V(lambda v: v.reciprocal(cols[:, 10:11], cols[:, 6:7]),
              waits=[(VE, ve_s2)], inc=VE)
            ve_rs2 = VE.n
            V(lambda v: v.tensor_scalar(r8b[:], r8a[:], cols[:, 10:11], None, ALU.mult),
              waits=[(VE, ve_rs2)])
            if dbg_cap:
                V(lambda v: v.tensor_copy(attsb[:, 32:40], r8a[:]))
                V(lambda v: v.tensor_copy(attsb[:, 40:48], r8b[:]))
                V(lambda v: v.tensor_copy(attsb[:, 48:56], sbc[:]))
                V(lambda v: v.tensor_copy(attsb[:, 56:64], cols[:, 8:16][:, 0:8]),
                  inc=VE)
                DMA("sync", QS, rchain_d[:], attsb[:, 0:64], waits=[(VE, VE.n)])
            ve_A = V(lambda v, i=i: v.tensor_mul(Asb[:, i * 8:(i + 1) * 8],
                                                 r8b[:], sbc[:]),
                     waits=[(VE, ve_sbc)], inc=VE)

        for nt in range(2):
            tabnext = p2_tables(nt, 0)
            for m in range(8):
                sinb2, cosb2, ae_tab2 = tabnext
                bank = pb[m % 2]
                for c in range(8):
                    MM(bank[:, 0:512],
                       qub[:, c * 1024 + m * 128: c * 1024 + m * 128 + 128],
                       xnT[:, c * 1024 + nt * 512: c * 1024 + nt * 512 + 512],
                       c == 0, False,
                       waits=([(QW, qw_attnw[(g, "qu")]), (QS, q_xnT), bfree(bank)]
                              if c == 0 else []))
                MM(bank[:, 0:512], bq_row[0:1, m * 128: m * 128 + 128],
                   ones_bf[0:1, 0:512], False, True, inc=PE)
                pe_q = PE.n
                if nt == 0:
                    emit_router(m)
                qsl = m % 2
                ksl2 = ksb[:, qsl * 512:(qsl + 1) * 512]
                ACT(ksl2, bank[:, 0:512], AF.Silu,
                    waits=[(PE, pe_q), lastuse.get(("qs", qsl))], inc=AE)
                bset(bank, (AE, AE.n))
                ae_q = AE.n
                V(lambda v, cb=cosb2, ksl2=ksl2: v.tensor_mul(
                    fsc1[:, 0:512], ksl2, cb),
                  waits=[(AE, max(ae_tab2, ae_q))])
                V(lambda v, sb=sinb2, ksl2=ksl2: v.tensor_mul(
                    fsc2[:, 0:512], ksl2, sb))
                for hh in range(2):
                    h = 2 * m + hh
                    V(lambda v, h=h, hh=hh: v.tensor_copy(
                        big1[0:64, h * 512:(h + 1) * 512],
                        fsc1[hh * 64:hh * 64 + 64, 0:512]),
                      waits=([lastuse.get(("big1",))] if hh == 0 else []))
                    V(lambda v, h=h, hh=hh: v.tensor_copy(
                        big1[64:128, h * 512:(h + 1) * 512],
                        fsc2[hh * 64:hh * 64 + 64, 0:512]),
                      inc=(VE if hh == 1 else None))
                s2 = m % 2
                tabslot_use[("p2", s2)] = (VE, VE.n)
                lastuse[("qs", qsl)] = (VE, VE.n)
                if m + 1 < 8:
                    tabnext = p2_tables(nt, m + 1)
            ve_qrot = VE.n
            for tsub in range(4):
                for dhh in range(2):
                    bank = pb[dhh]
                    for c in range(8):
                        MM(bank[:, 0:512],
                           xnT[:, c * 1024 + nt * 512 + tsub * 128:
                               c * 1024 + nt * 512 + tsub * 128 + 128],
                           qub[:, 8192 + c * 1024 + dhh * 512:
                               8192 + c * 1024 + dhh * 512 + 512],
                           c == 0, False, waits=([bfree(bank)] if c == 0 else []))
                    MM(bank[:, 0:512], ones_bf[0:1, 0:128],
                       bq_row[0:1, 3 * D + dhh * 512: 3 * D + dhh * 512 + 512],
                       False, True, inc=PE)
                    pe_u = PE.n
                    ACT(usb[:, tsub * 1024 + dhh * 512:
                            tsub * 1024 + dhh * 512 + 512],
                        bank[:, 0:512], AF.Copy,
                        waits=[(PE, pe_u), lastuse.get(("usb", tsub))], inc=AE)
                    bset(bank, (AE, AE.n))
            ae_u = AE.n
            for i2 in range(4):
                i = nt * 4 + i2
                ab = (pb[5], pb[6])
                for h in range(NH):
                    bank = ab[h // 8]
                    w = []
                    if h == 0:
                        w = [(VE, ve_qrot), (AE, ae_kv), bfree(ab[0]), bfree(ab[1])]
                    MM(bank[:, (h % 8) * 64:(h % 8) * 64 + 64],
                       big1[:, h * 512 + i2 * 128: h * 512 + i2 * 128 + 128],
                       kvb[:, sl * 1024 + h * 64: sl * 1024 + h * 64 + 64],
                       True, True, waits=w,
                       inc=(PE if h == NH - 1 else None))
                pe_att = PE.n
                slot = i % 2
                st = stats[:, slot * 12:(slot + 1) * 12]
                mvs = mv[:, slot * 2:(slot + 1) * 2]
                V(lambda v, st=st: v.bn_stats(st[:, 0:6], pb[5][:, 0:512]),
                  waits=[(PE, pe_att)])
                V(lambda v, st=st: v.bn_stats(st[:, 6:12], pb[6][:, 0:512]), inc=VE)
                ve_bs2 = VE.n
                V(lambda v, st=st, mvs=mvs: v.bn_aggr(mvs, st),
                  waits=[(VE, ve_bs2)], inc=VE)
                ve_mv = VE.n
                rsd = cols[:, 2 + slot:3 + slot]
                rsd2 = cols[:, 4 + slot:5 + slot]
                ACT(rsd, mvs[:, 1:2], AF.Sqrt, bias=cols[:, 0:1],
                    waits=[(VE, ve_mv)], inc=AE)
                ae_sq = AE.n
                V(lambda v, rsd=rsd, rsd2=rsd2: v.reciprocal(rsd2, rsd),
                  waits=[(AE, ae_sq)], inc=VE)
                ve_rs = VE.n
                V(lambda v, mvs=mvs, rsd2=rsd2: v.tensor_scalar(
                    fsc1[:, 0:512], pb[5][:, 0:512], mvs[:, 0:1], rsd2,
                    ALU.subtract, ALU.mult),
                  waits=[(VE, ve_rs)])
                V(lambda v, mvs=mvs, rsd2=rsd2: v.tensor_scalar(
                    fsc1[:, 512:1024], pb[6][:, 0:512], mvs[:, 0:1], rsd2,
                    ALU.subtract, ALU.mult), inc=VE)
                bset(ab[0], (VE, VE.n))
                bset(ab[1], (VE, VE.n))
                V(lambda v: v.tensor_mul(fsc2[:], fsc1[:], lnwbc[:]),
                  waits=[(VE, ve_lnwbc), lastuse.get(("tbA2",))])
                V(lambda v: v.tensor_add(fsc1[:], fsc2[:], lnbbc[:]),
                  waits=[(VE, ve_lnbbc)])
                gsl = gbf[:, slot * 1024:(slot + 1) * 1024]
                V(lambda v, i2=i2, gsl=gsl: v.tensor_mul(
                    gsl, fsc1[:], usb[:, i2 * 1024:(i2 + 1) * 1024]),
                  waits=[(AE, ae_u), lastuse.get(("gbf", slot))], inc=VE)
                ve_g = VE.n
                lastuse[("usb", i2)] = (VE, ve_g)
                for c in range(8):
                    TRANS(ptb[:, c * 128:(c + 1) * 128],
                          gsl[:, c * 128:(c + 1) * 128],
                          waits=([(VE, ve_g), (QS, consts_q),
                                  lastuse.get(("ptb",))] if c == 0 else []),
                          inc=(PE if c == 7 else None))
                pe_tr = PE.n
                gts = gT[:, slot * 1024:(slot + 1) * 1024]
                V(lambda v, gts=gts: v.tensor_copy(gts, ptb[:]),
                  waits=[(PE, pe_tr), lastuse.get(("gT", slot))], inc=VE)
                ve_gt = VE.n
                lastuse[("ptb",)] = (VE, ve_gt)
                lastuse[("gbf", slot)] = (PE, pe_tr)
                for dhh in range(2):
                    bank = pb[dhh]
                    for c in range(8):
                        MM(bank[:, 0:512], gts[:, c * 128:(c + 1) * 128],
                           wob[:, c * 1024 + dhh * 512: c * 1024 + dhh * 512 + 512],
                           c == 0, False,
                           waits=([(VE, ve_gt), (QW, qw_attnw[(g, "wo")]),
                                   bfree(bank)] if c == 0 else []))
                    MM(bank[:, 0:512], ones_bf[0:1, 0:128],
                       bo_row[0:1, dhh * 512: dhh * 512 + 512], False, True, inc=PE)
                    pe_wo = PE.n
                    aslice = acc[:, i * 1024 + dhh * 512: i * 1024 + dhh * 512 + 512]
                    V(lambda v, b=bank, aslice=aslice: v.scalar_tensor_tensor(
                        aslice, b[:, 0:512], 1.0, aslice, ALU.mult, ALU.add),
                      waits=[(PE, pe_wo), (QS, q_acc)], inc=VE)
                    bset(bank, (VE, VE.n))
                    acc_last_ve[i * 2 + dhh] = VE.n
                lastuse[("gT", slot)] = (PE, PE.n)
            lastuse[("big1",)] = (PE, PE.n)
        if debug and g == 0:
            DMA("sync", QS, A_d[:], Asb[:], waits=[(VE, ve_A)])
            for i in range(8):
                DMA("sync", QS, att_d[:, i * 1024:(i + 1) * 1024],
                    acc[:, i * 1024:(i + 1) * 1024],
                    waits=[(VE, max(acc_last_ve[i * 2], acc_last_ve[i * 2 + 1]))])
            V(lambda v: v.tensor_copy(attsb[:], kvb[:]), waits=[(AE, ae_kv)], inc=VE)
            DMA("sync", QS, kv_d[:], attsb[:], waits=[(VE, VE.n)])
            V(lambda v: v.tensor_copy(attsb[:], big1[:, 0:2048]), inc=VE)
            DMA("sync", QS, q_d[:, 0:2048], attsb[:], waits=[(VE, VE.n)])
        pe_attn_done = PE.n
        pe_last_w1b_use = (PE, pe_attn_done)
        pe_last_w2b_use = (PE, pe_attn_done)
        load_w1(g, 0, 0)
        load_w2(g, 0, 0)

        for e_ in range(E):
            for h_ in range(2):
                for tt in range(2):
                    sil_ae = []
                    for fc in range(16):
                        bank = pb[3 + fc % 2]
                        for c in range(8):
                            MM(bank[:, 0:512],
                               w1b[:, c * 2048 + fc * 128: c * 2048 + fc * 128 + 128],
                               xnT[:, c * 1024 + tt * 512: c * 1024 + tt * 512 + 512],
                               c == 0, c == 7,
                               waits=([(QW, qw_w1[(g, e_, h_)]), (QS, q_xnT),
                                       bfree(bank)] if c == 0 else []),
                               inc=(PE if c == 7 else None))
                        pe_h = PE.n
                        ACT(big1[:, fc * 512:(fc + 1) * 512], bank[:, 0:512], AF.Silu,
                            waits=[(PE, pe_h), lastuse.get(("big1",))], inc=AE)
                        bset(bank, (AE, AE.n))
                        sil_ae.append(AE.n)
                    ae_hdone = AE.n
                    pe_H_last = PE.n
                    if tt == 1:
                        ne, nh = (e_, 1) if h_ == 0 else (e_ + 1, 0)
                        pe_last_w1b_use = (PE, pe_H_last)
                        if ne < E:
                            load_w1(g, ne, nh)
                        elif g < 3:
                            load_attn_qu(g + 1)
                        if ne >= E:
                            lastuse[("xnT",)] = (PE, pe_H_last)
                    for tsub in range(4):
                        i = tt * 4 + tsub
                        for dhh in range(2):
                            bank = pb[5 + dhh]
                            for fc in range(16):
                                w = [(AE, sil_ae[fc])]
                                if fc == 0:
                                    w += [(QW, qw_w2[(g, e_, h_)]), bfree(bank)]
                                MM(bank[:, 0:512],
                                   big1[:, fc * 512 + tsub * 128:
                                        fc * 512 + tsub * 128 + 128],
                                   w2b[:, fc * 1024 + dhh * 512:
                                       fc * 1024 + dhh * 512 + 512],
                                   fc == 0, fc == 15,
                                   waits=w,
                                   inc=(PE if fc == 15 else None))
                            pe_y = PE.n
                            aslice = acc[:, i * 1024 + dhh * 512:
                                         i * 1024 + dhh * 512 + 512]
                            acol = Asb[:, i * 8 + e_: i * 8 + e_ + 1]
                            V(lambda v, b=bank, aslice=aslice, acol=acol:
                              v.scalar_tensor_tensor(aslice, b[:, 0:512], acol,
                                                     aslice, ALU.mult, ALU.add),
                              waits=[(PE, pe_y), (VE, ve_A)], inc=VE)
                            bset(bank, (VE, VE.n))
                            acc_last_ve[i * 2 + dhh] = VE.n
                    lastuse[("big1",)] = (PE, PE.n)
                ne, nh = (e_, 1) if h_ == 0 else (e_ + 1, 0)
                pe_last_w2b_use = (PE, PE.n)
                if ne < E:
                    load_w2(g, ne, nh)
                elif g < 3:
                    load_attn_wo(g + 1)

        for i in range(8):
            acc_store_q = DMA(
                "sync", QS, out_c[tb + i * 128: tb + i * 128 + 128, :],
                acc[:, i * 1024:(i + 1) * 1024],
                waits=[(VE, max(acc_last_ve[i * 2], acc_last_ve[i * 2 + 1]))])

    _op("sync", lambda s_: s_.nop(), waits=[(QS, QS.n)])

    with nc.Block() as block:
        @block.sync
        def _(sync):
            for f in P["sync"]:
                f(sync)

        @block.scalar
        def _(scalar):
            for f in P["scalar"]:
                f(scalar)

        @block.vector
        def _(vector):
            for f in P["vector"]:
                f(vector)

        @block.tensor
        def _(tensor):
            for f in P["tensor"]:
                f(tensor)

        @block.gpsimd
        def _(gpsimd):
            for f in P["gpsimd"]:
                f(gpsimd)

    es.close()
    return nc


_CACHE = {}


def kernel(**inputs):
    _ = inputs.get("L0_attn_context")
    x = np.ascontiguousarray(np.asarray(inputs["L0_attn_x"], dtype=np.float32))
    wq = np.ascontiguousarray(np.asarray(inputs["L0_attn_Wqkvu"],
                                         dtype=np.float32).T)
    bq = np.ascontiguousarray(np.asarray(inputs["L0_attn_bqkvu"],
                                         dtype=np.float32)).reshape(1, -1)
    wo = np.ascontiguousarray(np.asarray(inputs["L0_attn_Wout"],
                                         dtype=np.float32).T)
    bo = np.ascontiguousarray(np.asarray(inputs["L0_attn_bout"],
                                         dtype=np.float32)).reshape(1, -1)
    theta = np.ascontiguousarray(np.asarray(inputs["L0_attn_theta"],
                                            dtype=np.float32)).reshape(-1)
    lnw = np.ascontiguousarray(np.asarray(inputs["L0_attn_ln_w"],
                                          dtype=np.float32)).reshape(1, -1)
    lnb = np.ascontiguousarray(np.asarray(inputs["L0_attn_ln_b"],
                                          dtype=np.float32)).reshape(1, -1)
    rwm = np.ascontiguousarray(np.asarray(inputs["L0_ffn_router_w"], dtype=np.float32))
    w1 = np.ascontiguousarray(np.asarray(inputs["L0_ffn_w1"],
                                         dtype=np.float32).transpose(0, 2, 1))
    w2 = np.ascontiguousarray(np.asarray(inputs["L0_ffn_w2"],
                                         dtype=np.float32).transpose(0, 2, 1))
    esc = np.ascontiguousarray(np.asarray(inputs["L0_ffn_expert_scale"],
                                          dtype=np.float32)).reshape(1, -1)

    th_row = np.ascontiguousarray(theta.reshape(1, -1))
    thcol = np.ascontiguousarray(theta.reshape(8, 128).T)
    rwT = np.ascontiguousarray(rwm.T)
    eye = np.eye(128, dtype=ml_dtypes.bfloat16)

    if "nc" not in _CACHE:
        _CACHE["nc"] = build_nc()
    nc = _CACHE["nc"]

    in_maps = []
    for c in range(NCORES):
        xc = np.ascontiguousarray(x[2 * c:2 * c + 2].reshape(T, D))
        in_maps.append({
            "x_c": xc, "wq_f": wq, "wout_f": wo, "rw_f": rwT,
            "w1_f": w1, "w2_f": w2, "bq_row": bq, "bo_row": bo,
            "th_row": th_row, "thcol": thcol, "lnw_row": lnw, "lnb_row": lnb,
            "sc_row": esc, "eye_d": eye,
        })
    res = run_bass_kernel_spmd(nc, in_maps, core_ids=list(range(NCORES)))
    out = np.empty((16, SEQ, D), dtype=np.float32)
    for c in range(NCORES):
        out[2 * c:2 * c + 2] = res.results[c]["out_c"].reshape(2, SEQ, D)
    return out


# version 30 (final)
# speedup vs baseline: 1.2001x; 1.1993x over previous
import numpy as np
import ml_dtypes
import concourse.bass as bass
import concourse.mybir as mybir
from concourse.bass_utils import run_bass_kernel_spmd
from contextlib import ExitStack

F32 = mybir.dt.float32
BF16 = mybir.dt.bfloat16
I32 = mybir.dt.int32
ALU = mybir.AluOpType
AF = mybir.ActivationFunctionType
AX = mybir.AxisListType.X

NCORES = 8
SEQ = 2048
D = 1024
NH = 16
DH = 64
E = 8
FD = 4096
T = 2 * SEQ
EPS = 1e-5
TWO_PI = float(2.0 * np.pi)
INV2PI = float(1.0 / (2.0 * np.pi))
HALF_PI = float(np.pi / 2.0)

ENGS = ("sync", "scalar", "vector", "tensor", "gpsimd")


class Sm:
    def __init__(self, h):
        self.h = h
        self.n = 0

    def bump(self, k=1):
        self.n += k
        return self.n


def build_nc(debug=False, audit=None):
    nc = bass.Bass("TRN2", target_bir_lowering=False)

    x_c = nc.dram_tensor("x_c", [T, D], F32, kind="ExternalInput")
    wq_f = nc.dram_tensor("wq_f", [D, 4 * D], F32, kind="ExternalInput")
    wout_f = nc.dram_tensor("wout_f", [D, D], F32, kind="ExternalInput")
    rw_f = nc.dram_tensor("rw_f", [D, E], F32, kind="ExternalInput")
    w1_f = nc.dram_tensor("w1_f", [E, D, FD], F32, kind="ExternalInput")
    w2_f = nc.dram_tensor("w2_f", [E, FD, D], F32, kind="ExternalInput")
    bq_row_d = nc.dram_tensor("bq_row", [1, 4 * D], F32, kind="ExternalInput")
    bo_row_d = nc.dram_tensor("bo_row", [1, D], F32, kind="ExternalInput")
    th_row_d = nc.dram_tensor("th_row", [1, D], F32, kind="ExternalInput")
    thcol_d = nc.dram_tensor("thcol", [128, 8], F32, kind="ExternalInput")
    lnw_row_d = nc.dram_tensor("lnw_row", [1, D], F32, kind="ExternalInput")
    lnb_row_d = nc.dram_tensor("lnb_row", [1, D], F32, kind="ExternalInput")
    sc_row_d = nc.dram_tensor("sc_row", [1, E], F32, kind="ExternalInput")
    eye_d = nc.dram_tensor("eye_d", [128, 128], BF16, kind="ExternalInput")
    out_c = nc.dram_tensor("out_c", [T, D], F32, kind="ExternalOutput")

    xnsT = nc.dram_tensor("xnsT", [D, T], BF16, kind="Internal")
    wqs = nc.dram_tensor("wqs", [D, 4 * D], BF16, kind="Internal")
    wouts = nc.dram_tensor("wouts", [D, D], BF16, kind="Internal")
    rws = nc.dram_tensor("rws", [D, E], BF16, kind="Internal")
    w1s = nc.dram_tensor("w1s", [E, D, FD], BF16, kind="Internal")
    w2s = nc.dram_tensor("w2s", [E, FD, D], BF16, kind="Internal")
    if debug:
        xns_d = nc.dram_tensor("xns_d", [D, T], BF16, kind="ExternalOutput")
        kv_d = nc.dram_tensor("kv_d", [128, 2048], F32, kind="ExternalOutput")
        A_d = nc.dram_tensor("A_d", [128, 64], F32, kind="ExternalOutput")
        att_d = nc.dram_tensor("att_d", [128, 8192], F32, kind="ExternalOutput")
        q_d = nc.dram_tensor("q_d", [128, 8192], F32, kind="ExternalOutput")
        xnt_d = nc.dram_tensor("xnt_d", [128, 8192], BF16, kind="ExternalOutput")
        iob_d = nc.dram_tensor("iob_d", [128, 1024], F32, kind="ExternalOutput")
        tab_d = nc.dram_tensor("tab_d", [128, 2048], F32, kind="ExternalOutput")
        qs_d = nc.dram_tensor("qs_d", [128, 512], F32, kind="ExternalOutput")
        tb1_d = nc.dram_tensor("tb1_d", [128, 2048], F32, kind="ExternalOutput")
        kr_d = nc.dram_tensor("kr_d", [128, 2048], F32, kind="ExternalOutput")
        kd_d = nc.dram_tensor("kd_d", [128, 1024], F32, kind="ExternalOutput")
        vd_d = nc.dram_tensor("vd_d", [128, 1024], F32, kind="ExternalOutput")
        lg_d = nc.dram_tensor("lg_d", [128, 8], F32, kind="ExternalOutput")
        rw_d = nc.dram_tensor("rw_d", [128, 64], F32, kind="ExternalOutput")
        rchain_d = nc.dram_tensor("rchain_d", [128, 64], F32, kind="ExternalOutput")

    es = ExitStack()
    A = es.enter_context

    acc = A(nc.sbuf_tensor("acc", [128, 8 * D], F32))
    xnT = A(nc.sbuf_tensor("xnT", [128, 8 * D], BF16))
    big1 = A(nc.sbuf_tensor("big1", [128, 16 * 512], BF16))
    w1b = A(nc.sbuf_tensor("w1b", [128, 8 * 2048], BF16))
    w2b = A(nc.sbuf_tensor("w2b", [128, 16 * 1024], BF16))
    kvb = A(nc.sbuf_tensor("kvb", [128, 2048], BF16))
    usb = A(nc.sbuf_tensor("usb", [128, 4096], BF16))
    gbf = A(nc.sbuf_tensor("gbf", [128, 2048], BF16))
    gT = A(nc.sbuf_tensor("gT", [128, 2048], BF16))
    eye = A(nc.sbuf_tensor("eye", [128, 128], BF16))
    rw = A(nc.sbuf_tensor("rw", [128, 64], BF16))

    attsb = A(nc.sbuf_tensor("attsb", [128, 2048], F32))
    fsc1 = A(nc.sbuf_tensor("fsc1", [128, 1024], F32))
    fsc2 = A(nc.sbuf_tensor("fsc2", [128, 1024], F32))
    ksb = A(nc.sbuf_tensor("ksb", [128, 1024], F32))
    tbarg = A(nc.sbuf_tensor("tbarg", [128, 1024], F32))
    tbsin = A(nc.sbuf_tensor("tbsin", [128, 1024], BF16))
    tbcos = A(nc.sbuf_tensor("tbcos", [128, 1024], BF16))
    shbc1 = A(nc.sbuf_tensor("shbc1", [128, 1024], F32))
    lnbbc = A(nc.sbuf_tensor("lnbbc", [128, 1024], F32))
    iotabc = A(nc.sbuf_tensor("iotabc", [128, 1024], F32))
    psc = A(nc.sbuf_tensor("psc", [128, 1024], F32))
    ki = A(nc.sbuf_tensor("ki", [128, 512], I32))
    stats = A(nc.sbuf_tensor("stats", [128, 24], F32))
    mv = A(nc.sbuf_tensor("mv", [128, 4], F32))
    cols = A(nc.sbuf_tensor("cols", [128, 16], F32))
    Asb = A(nc.sbuf_tensor("Asb", [128, 64], F32))
    r8a = A(nc.sbuf_tensor("r8a", [128, 8], F32))
    r8b = A(nc.sbuf_tensor("r8b", [128, 8], F32))
    r8c = A(nc.sbuf_tensor("r8c", [128, 8], F32))
    sbc = A(nc.sbuf_tensor("sbc", [128, 8], F32))
    thcol = A(nc.sbuf_tensor("thcol_s", [128, 8], F32))
    poscol = A(nc.sbuf_tensor("poscol", [128, 16], F32))
    bq_row = A(nc.sbuf_tensor("bq_row_s", [1, 4 * D], BF16))
    bo_row = A(nc.sbuf_tensor("bo_row_s", [1, D], BF16))
    ones_r = A(nc.sbuf_tensor("ones_r", [1, 128], F32))
    row_s = attsb[0:1, 0:1024]
    ones_bf = A(nc.sbuf_tensor("ones_bf", [1, 512], BF16))
    krot = usb[:, 0:2048]
    vsb = usb[:, 2048:3072]
    vsb2 = usb[:, 3072:4096]
    thetabc = shbc1
    lnwbc = shbc1

    pb = [A(nc.psum_tensor(f"pb{i}", [128, 512], F32)) for i in range(7)]
    ptb = A(nc.psum_tensor("ptb", [128, 1024], BF16))

    QS = Sm(A(nc.semaphore("qs")))
    QW = Sm(A(nc.semaphore("qw")))
    SC = Sm(A(nc.semaphore("sc")))
    KL = Sm(A(nc.semaphore("kl")))
    XS = Sm(A(nc.semaphore("xs")))
    PE = Sm(A(nc.semaphore("pe")))
    VE = Sm(A(nc.semaphore("ve")))
    AE = Sm(A(nc.semaphore("ae")))
    GE = Sm(A(nc.semaphore("ge")))

    P = {e: [] for e in ENGS}
    WAITED = {e: {} for e in ENGS}

    def _op(eng, build, waits=(), inc=None, inck=1):
        wl = []
        for w in waits:
            if w is None:
                continue
            sem, tgt = w
            if tgt is None or tgt <= 0:
                continue
            prev = WAITED[eng].get(id(sem), 0)
            if tgt > prev:
                WAITED[eng][id(sem)] = tgt
                wl.append((sem.h, tgt))
        ih = None
        if inc is not None:
            inc.bump(inck)
            ih, iv = inc.h, inck
        if audit is not None:
            audit.append((eng, [(id(s_), t_) for s_, t_ in
                                [(w[0], w[1]) for w in waits
                                 if w is not None and w[1] and w[1] > 0]],
                          (id(inc), inck) if inc is not None else None,
                          len(audit)))

        def f(eng_obj):
            for h, t in wl:
                eng_obj.wait_ge(h, t)
            ins = build(eng_obj)
            if ih is not None:
                ins.then_inc(ih, iv)
        P[eng].append(f)
        return inc.n if inc is not None else None

    def DMA(eng, qsem, out_ap, in_ap, waits=(), transpose=False, **kw):
        _op(eng, lambda o: o.dma_start(out_ap, in_ap, transpose=transpose, **kw),
            waits, inc=qsem, inck=16)
        return qsem.n

    def MM(out, l, r, start, stop, waits=(), inc=None, skip=False):
        return _op("tensor",
                   lambda t: t.matmul(out, l, r, start=start, stop=stop,
                                      skip_group_check=skip),
                   waits, inc)

    def TRANS(out, in_, waits=(), inc=None):
        return _op("tensor", lambda t: t.transpose(out, in_, eye[:]), waits, inc)

    def ACT(out, in_, func, bias=None, scale=None, accum=None, waits=(), inc=None):
        def b(s_):
            kw = {}
            if bias is not None:
                kw["bias"] = bias
            if scale is not None:
                kw["scale"] = scale
            if accum is not None:
                kw["accum_out"] = accum
            return s_.activation(out, in_, func, **kw)
        return _op("scalar", b, waits, inc)

    def V(fn, waits=(), inc=None):
        w = list(waits) + [(VE, VE.n)]
        return _op("vector", fn, w, inc=VE)

    def G(fn, waits=(), inc=None):
        w = list(waits) + [(GE, GE.n)]
        return _op("gpsimd", fn, w, inc=GE)

    bankfree = {}

    def bfree(bank):
        return bankfree.get(id(bank))

    def bset(bank, semcount):
        bankfree[id(bank)] = semcount

    lastuse = {}

    DMA("sync", QS, eye[:], eye_d[:])
    DMA("sync", QS, thcol[:], thcol_d[:])
    DMA("sync", QS, row_s, th_row_d[:])
    q_sc = DMA("sync", QS, lnbbc[0:1, 0:E], sc_row_d[:])
    q_throw = q_sc
    consts_q = QS.n
    DMA("gpsimd", SC, bq_row[:], bq_row_d[:])
    DMA("gpsimd", SC, bo_row[:], bo_row_d[:])
    sc_bias = SC.n

    for pc in range(16):
        G(lambda g, pc=pc: g.iota(poscol[:, pc:pc + 1], [[1, 1]], base=pc * 128,
                                  channel_multiplier=1,
                                  allow_small_or_imprecise_dtypes=True))
    G(lambda g: g.iota(iotabc[:], [[1, 1024]], base=0,
                       channel_multiplier=0,
                       allow_small_or_imprecise_dtypes=True))
    G(lambda g: g.engine_nop(), inc=GE)
    ge_iota = GE.n

    DMA("gpsimd", SC, rws[:], rw_f[:], max_dma_last_dim=8192,
        waits=[(SC, sc_bias)])
    sc_router = SC.n
    DMA("gpsimd", SC, wqs[:, 0:D], wq_f[:, 0:D], max_dma_last_dim=8192,
        waits=[(SC, sc_router)])
    DMA("gpsimd", SC, wqs[:, 3 * D:4 * D], wq_f[:, 3 * D:4 * D],
        max_dma_last_dim=8192)
    sc_qkvu = SC.n
    DMA("gpsimd", SC, wouts[:], wout_f[:], max_dma_last_dim=8192,
        waits=[(SC, sc_qkvu)])
    sc_wout = SC.n
    sc_prev = sc_wout
    sc_w1 = {}
    sc_w2 = {}
    for e_ in range(E):
        for h_ in range(2):
            DMA("gpsimd", SC, w1s[e_, :, h_ * 2048:(h_ + 1) * 2048],
                w1_f[e_, :, h_ * 2048:(h_ + 1) * 2048], max_dma_last_dim=8192,
                waits=[(SC, sc_prev)])
            sc_w1[(e_, h_)] = SC.n
            sc_prev = SC.n
            DMA("gpsimd", SC, w2s[e_, h_ * 2048:(h_ + 1) * 2048, :],
                w2_f[e_, h_ * 2048:(h_ + 1) * 2048, :], max_dma_last_dim=8192,
                waits=[(SC, sc_prev)])
            sc_w2[(e_, h_)] = SC.n
            sc_prev = SC.n

    V(lambda v: v.memset(ones_r[:], 1.0))
    V(lambda v: v.memset(ones_bf[:], 1.0))
    V(lambda v: v.memset(cols[:, 0:1], EPS))
    V(lambda v: v.memset(cols[:, 1:2], HALF_PI))
    V(lambda v: v.memset(cols[:, 11:12], -1.0), inc=VE)
    ve_memset = VE.n

    def bcast_row(dst_sbuf, src_row_ap, ncols, waits_pe):
        last = None
        for j in range(0, ncols, 512):
            bank = pb[2 + (j // 512) % 2]
            MM(bank[:, 0:512], ones_r[0:1, 0:128], src_row_ap[0:1, j:j + 512],
               True, True, waits=list(waits_pe) + [bfree(bank)], inc=PE)
            pe_n = PE.n
            V(lambda v, b=bank, j=j: v.tensor_copy(dst_sbuf[:, j:j + 512], b[:, 0:512]),
              waits=[(PE, pe_n)], inc=VE)
            bset(bank, (VE, VE.n))
            last = VE.n
        return last

    ve_thetabc = bcast_row(thetabc, row_s, D, [(QS, q_throw), (VE, ve_memset)])
    bank = pb[2]
    MM(bank[:, 0:8], ones_r[0:1, 0:128], lnbbc[0:1, 0:E], True, True,
       waits=[(QS, q_sc), bfree(bank)], inc=PE)
    pe_sbc = PE.n
    V(lambda v, b=bank: v.tensor_copy(sbc[:], b[:, 0:8]), waits=[(PE, pe_sbc)], inc=VE)
    bset(bank, (VE, VE.n))
    ve_sbc = VE.n
    lastuse[("xa", 0)] = (PE, PE.n)
    lastuse[("xa", 1)] = (VE, ve_memset)

    DMA("scalar", KL, acc[:].rearrange("p (c n) -> p c n", c=8),
        wq_f[:, D:2 * D].rearrange("(c p) n -> p c n", p=128))
    kl_k = KL.n
    ACT(w1b[:, 0:8192], acc[:], AF.Copy, waits=[(KL, kl_k)], inc=AE)
    ae_kvk = AE.n
    DMA("scalar", KL, acc[:].rearrange("p (c n) -> p c n", c=8),
        wq_f[:, 2 * D:3 * D].rearrange("(c p) n -> p c n", p=128),
        waits=[(AE, ae_kvk)])
    kl_v = KL.n
    ACT(w1b[:, 8192:16384], acc[:], AF.Copy, waits=[(KL, kl_v)], inc=AE)
    ae_kvv = AE.n
    lastuse[("acc",)] = (AE, ae_kvv)

    qw_rw = DMA("scalar", QW, rw[:].rearrange("p (c e) -> p c e", e=8),
                rws[:].rearrange("(c p) e -> p c e", p=128), waits=[(SC, sc_router)])

    xstore_q = [0] * 32
    kv_first = [True] * 4
    tabslot_use = {}

    def p1_tables(pc):
        s = pc % 2
        sinb = tbsin[:] if s == 0 else big1[:, 0:1024]
        cosb = tbcos[:] if s == 0 else big1[:, 1024:2048]
        V(lambda v, pc=pc: v.tensor_scalar(tbarg[:], thetabc[:], poscol[:, pc:pc + 1],
                                           None, ALU.mult),
          waits=[(VE, ve_thetabc), (GE, ge_iota), lastuse.get(("tbA",))])
        V(lambda v: v.tensor_scalar(fsc1[:, 0:512], tbarg[:, 0:512], INV2PI,
                                    None, ALU.mult))
        V(lambda v: v.tensor_copy(ki[:], fsc1[:, 0:512]))
        V(lambda v: v.tensor_copy(fsc1[:, 0:512], ki[:]))
        V(lambda v: v.scalar_tensor_tensor(fsc2[:, 0:512], fsc1[:, 0:512], -TWO_PI,
                                           tbarg[:, 0:512], ALU.mult, ALU.add))
        V(lambda v: v.tensor_scalar(fsc1[:, 0:512], tbarg[:, 0:512], INV2PI, 0.25,
                                    ALU.mult, ALU.add))
        V(lambda v: v.tensor_copy(ki[:], fsc1[:, 0:512]))
        V(lambda v: v.tensor_copy(fsc1[:, 0:512], ki[:]))
        V(lambda v: v.scalar_tensor_tensor(fsc2[:, 512:1024], fsc1[:, 0:512], -TWO_PI,
                                           tbarg[:, 0:512], ALU.mult, ALU.add),
          inc=VE)
        ve_args = VE.n
        sw = tabslot_use.get((s, "sin"))
        cw = tabslot_use.get((s, "cos"))
        ACT(sinb[:, 0:512], fsc2[:, 0:512], AF.Sin, waits=[(VE, ve_args), sw, cw])
        ACT(cosb[:, 0:512], fsc2[:, 512:1024], AF.Sin, bias=cols[:, 1:2])
        ACT(sinb[:, 512:1024], tbarg[:, 512:1024], AF.Sin)
        ACT(cosb[:, 512:1024], tbarg[:, 512:1024], AF.Sin,
            bias=cols[:, 1:2], inc=AE)
        lastuse[("tbA",)] = (AE, AE.n)
        return sinb, cosb, AE.n

    tiles = [(pc, sl) for pc in range(16) for sl in range(2)]
    q_x = [0] * 32

    def p1_load(i):
        pc, sl = tiles[i]
        tr = sl * SEQ + pc * 128
        slot = i % 2
        xa = attsb[:, slot * 1024:(slot + 1) * 1024]
        q_x[i] = DMA("sync", QS, xa, x_c[tr:tr + 128, :],
                     waits=[lastuse.get(("xa", slot))])
        _op("sync", lambda s_: s_.nop())

    def p1_ln(i, tab):
        pc, sl = tiles[i]
        tc = sl * 16 + pc
        tr = sl * SEQ + pc * 128
        slot = i % 2
        xa = attsb[:, slot * 1024:(slot + 1) * 1024]
        qx = q_x[i]
        st = stats[:, slot * 12:(slot + 1) * 12]
        mvs = mv[:, slot * 2:(slot + 1) * 2]
        V(lambda v, xa=xa, st=st: v.bn_stats(st[:, 0:6], xa[:, 0:512]),
          waits=[(QS, qx)])
        V(lambda v, xa=xa, st=st: v.bn_stats(st[:, 6:12], xa[:, 512:1024]), inc=VE)
        ve_bs = VE.n
        V(lambda v, st=st, mvs=mvs: v.bn_aggr(mvs, st), waits=[(VE, ve_bs)], inc=VE)
        ve_mv = VE.n
        rsd = cols[:, 2 + slot:3 + slot]
        rsd2 = cols[:, 4 + slot:5 + slot]
        ACT(rsd, mvs[:, 1:2], AF.Sqrt, bias=cols[:, 0:1], waits=[(VE, ve_mv)], inc=AE)
        ae_s = AE.n
        V(lambda v, rsd=rsd, rsd2=rsd2: v.reciprocal(rsd2, rsd),
          waits=[(AE, ae_s)], inc=VE)
        ve_r = VE.n
        xw = gbf[:, slot * 1024:(slot + 1) * 1024]
        V(lambda v, xa=xa, mvs=mvs, rsd2=rsd2, xw=xw:
          v.tensor_scalar(xw, xa, mvs[:, 0:1], rsd2, ALU.subtract, ALU.mult),
          waits=[(VE, ve_r), lastuse.get(("gbf", slot))], inc=VE)
        ve_xn = VE.n
        lastuse[("xa", slot)] = (VE, ve_xn)
        lastuse[("attsb", slot)] = (VE, ve_xn)
        return dict(pc=pc, sl=sl, slot=slot, tc=tc, tr=tr, ve_xn=ve_xn, xw=xw,
                    tab=tab)

    def p1_rest(stt_):
        pc, sl, slot = stt_["pc"], stt_["sl"], stt_["slot"]
        tc, tr = stt_["tc"], stt_["tr"]
        xw, ve_xn = stt_["xw"], stt_["ve_xn"]
        sinb, cosb, ae_tab1 = stt_["tab"]
        s = pc % 2
        xp = gT[:, slot * 1024:(slot + 1) * 1024]
        for c in range(8):
            TRANS(ptb[:, c * 128:(c + 1) * 128], xw[:, c * 128:(c + 1) * 128],
                  waits=([(VE, ve_xn), (QS, consts_q), lastuse.get(("ptb",))]
                         if c == 0 else []),
                  inc=(PE if c == 7 else None))
        pe_tp = PE.n
        lastuse[("gbf", slot)] = (PE, pe_tp)
        ACT(xp, ptb[:], AF.Copy, waits=[(PE, pe_tp), lastuse.get(("gTa", slot)),
                                        lastuse.get(("gTq", slot))],
            inc=AE)
        ae_xp = AE.n
        lastuse[("ptb",)] = (AE, ae_xp)
        qst = DMA("scalar", XS,
                  xnsT.rearrange("(c p) t -> p c t", p=128)[:, :, tr:tr + 128],
                  xp.rearrange("p (c t) -> p c t", t=128),
                  waits=[(AE, ae_xp)])
        xstore_q[tc] = qst
        lastuse[("gTq", slot)] = (XS, qst)
        ksl = ksb if slot == 0 else psc
        vsl = vsb if slot == 0 else vsb2
        pe_k = [0, 0]
        for half in range(2):
            bank = pb[half]
            for c in range(8):
                MM(bank[:, 0:512], xp[:, c * 128:(c + 1) * 128],
                   w1b[:, c * 1024 + half * 512: c * 1024 + half * 512 + 512],
                   c == 0, False,
                   waits=([(AE, ae_xp), (AE, ae_kvk), bfree(bank)]
                          if c == 0 else []))
            MM(bank[:, 0:512], ones_bf[0:1, 0:128],
               bq_row[0:1, D + half * 512: D + half * 512 + 512],
               False, True, waits=[(QS, consts_q), (SC, sc_bias)], inc=PE)
            pe_k[half] = PE.n
        for half in range(2):
            bank = pb[half]
            ACT(ksl[:, half * 512:(half + 1) * 512], bank[:, 0:512], AF.Silu,
                waits=[(PE, pe_k[half]), lastuse.get(("ksb", slot))], inc=AE)
            bset(bank, (AE, AE.n))
        ae_k = AE.n
        V(lambda v, cosb=cosb, ksl=ksl: v.tensor_mul(fsc1[:], ksl[:], cosb),
          waits=[(AE, max(ae_k, ae_tab1)), lastuse.get(("krot",)),
                 lastuse.get(("tbA",))])
        V(lambda v, sinb=sinb, ksl=ksl: v.tensor_mul(fsc2[:], ksl[:], sinb))
        kr4 = krot[:].rearrange("p (h two dh) -> p h two dh", two=2, dh=DH)
        V(lambda v, kr4=kr4: v.tensor_copy(
            kr4[:, :, 0, :], fsc1[:].rearrange("p (h dh) -> p h dh", dh=DH)))
        V(lambda v, kr4=kr4: v.tensor_copy(
            kr4[:, :, 1, :], fsc2[:].rearrange("p (h dh) -> p h dh", dh=DH)),
          inc=VE)
        ve_krot = VE.n
        lastuse[("ksb", slot)] = (VE, ve_krot)
        tabslot_use[(s, "sin")] = (VE, ve_krot)
        tabslot_use[(s, "cos")] = (VE, ve_krot)
        pe_v = [0, 0]
        for half in range(2):
            bank = pb[half]
            for c in range(8):
                MM(bank[:, 0:512], xp[:, c * 128:(c + 1) * 128],
                   w1b[:, 8192 + c * 1024 + half * 512:
                       8192 + c * 1024 + half * 512 + 512],
                   c == 0, False,
                   waits=([(AE, ae_kvv), bfree(bank)] if c == 0 else []))
            MM(bank[:, 0:512], ones_bf[0:1, 0:128],
               bq_row[0:1, 2 * D + half * 512: 2 * D + half * 512 + 512],
               False, True, inc=PE)
            pe_v[half] = PE.n
        for half in range(2):
            bank = pb[half]
            ACT(vsl[:, half * 512:(half + 1) * 512], bank[:, 0:512], AF.Copy,
                waits=[(PE, pe_v[half]), lastuse.get(("vsb", slot))], inc=AE)
            bset(bank, (AE, AE.n))
        ae_v = AE.n
        for h in range(NH):
            kvbank = pb[3 + sl * 2 + h // 8]
            bi = sl * 2 + h // 8
            first = kv_first[bi]
            kv_first[bi] = False
            w = []
            if h == 0:
                w = [(VE, ve_krot), (AE, ae_v)]
                if first:
                    w += [bfree(pb[3 + sl * 2]), bfree(pb[4 + sl * 2])]
            MM(kvbank[:, (h % 8) * 64:(h % 8) * 64 + 64],
               krot[:, h * 128:(h + 1) * 128], vsl[:, h * 64:(h + 1) * 64],
               first, pc == 15, waits=w, skip=True,
               inc=(PE if h == NH - 1 else None))
        lastuse[("krot",)] = (PE, PE.n)
        lastuse[("vsb", slot)] = (PE, PE.n)
        lastuse[("gTa", slot)] = (PE, PE.n)

    p1_load(0)
    p1_load(1)
    tabs = {0: p1_tables(0)}
    lns = [None] * 32
    lns[0] = p1_ln(0, tabs[0])
    qw_qu0 = None
    for i in range(32):
        if i + 2 < 32:
            p1_load(i + 2)
        if i + 1 < 32:
            if (i + 1) % 2 == 0:
                tabs[(i + 1) // 2] = p1_tables((i + 1) // 2)
            lns[i + 1] = p1_ln(i + 1, tabs[(i + 1) // 2])
        p1_rest(lns[i])
    pe_kv_done = PE.n
    lastuse[("gT", 0)] = (XS, xstore_q[15])
    lastuse[("gT", 1)] = (XS, xstore_q[31])
    for sl in range(2):
        for bi in range(2):
            bank = pb[3 + sl * 2 + bi]
            ACT(kvb[:, sl * 1024 + bi * 512: sl * 1024 + bi * 512 + 512],
                bank[:, 0:512], AF.Copy, waits=[(PE, pe_kv_done)], inc=AE)
            bset(bank, (AE, AE.n))
    ae_kv = AE.n
    lastuse[("qs", 0)] = (VE, VE.n)
    lastuse[("qs", 1)] = (VE, VE.n)
    lastuse[("big1",)] = (AE, AE.n)
    for _ts in range(4):
        lastuse[("usb", _ts)] = (PE, pe_kv_done)
    q_lnw = DMA("sync", QS, row_s, lnw_row_d[:], waits=[(PE, PE.n), (VE, VE.n)])
    ve_lnwbc = bcast_row(lnwbc, row_s, D, [(QS, q_lnw), (VE, VE.n), (AE, ae_kv)])
    q_lnb = DMA("sync", QS, row_s, lnb_row_d[:], waits=[(PE, PE.n)])
    ve_lnbbc = bcast_row(lnbbc, row_s, D, [(QS, q_lnb)])
    lastuse[("attsb", 0)] = (PE, PE.n)
    if debug:
        DMA("sync", QS, xns_d[:], xnsT[:], waits=[(XS, max(xstore_q))])
        V(lambda v: v.tensor_copy(attsb[:], big1[:, 0:2048]),
          waits=[(PE, pe_kv_done)], inc=VE)
        q_tb1 = DMA("sync", QS, tb1_d[:], attsb[:], waits=[(VE, VE.n)])
        V(lambda v: v.tensor_copy(attsb[:], krot[:]),
          waits=[(QS, q_tb1)], inc=VE)
        q_tb1 = DMA("sync", QS, kr_d[:], attsb[:], waits=[(VE, VE.n)])
        V(lambda v: v.tensor_copy(attsb[:, 0:1024], psc[:]),
          waits=[(QS, q_tb1)])
        V(lambda v: v.tensor_copy(attsb[:, 1024:2048], vsb2[:]), inc=VE)
        q_tb1 = DMA("sync", QS, kd_d[:], attsb[:, 0:1024], waits=[(VE, VE.n)])
        q_tb1 = DMA("sync", QS, vd_d[:], attsb[:, 1024:2048])
        lastuse[("attsb", 0)] = (QS, q_tb1)
        lastuse[("attsb", 1)] = (QS, q_tb1)

    pe_last_w1b_use = (PE, pe_kv_done)
    pe_last_w2b_use = None
    qw_attnw = {}
    qw_w1 = {}
    qw_w2 = {}
    acc_store_q = 0
    acc_last_ve = [0] * 16
    prev_ae_ep = 0

    def load_attn_qu(g):
        DMA("scalar", QW, w1b[:, 0:8192].rearrange("p (c n) -> p c n", c=8),
            wqs[:, 0:D].rearrange("(c p) n -> p c n", p=128),
            waits=[pe_last_w1b_use, (SC, sc_qkvu)])
        last = DMA("scalar", QW,
                   w1b[:, 8192:16384].rearrange("p (c n) -> p c n", c=8),
                   wqs[:, 3 * D:4 * D].rearrange("(c p) n -> p c n", p=128))
        qw_attnw[(g, "qu")] = last
        _op("scalar", lambda s_: s_.nop())

    def load_attn_wo(g):
        last = DMA("scalar", QW,
                   w2b[:, 0:8192].rearrange("p (c n) -> p c n", c=8),
                   wouts[:].rearrange("(c p) n -> p c n", p=128),
                   waits=[pe_last_w2b_use, (SC, sc_wout)])
        qw_attnw[(g, "wo")] = last
        _op("scalar", lambda s_: s_.nop())

    def load_w1(g, e_, h_):
        last = DMA("scalar", QW, w1b[:].rearrange("p (c n) -> p c n", c=8),
                   w1s[e_, :, h_ * 2048:(h_ + 1) * 2048]
                   .rearrange("(c p) n -> p c n", p=128),
                   waits=[pe_last_w1b_use, (SC, sc_w1[(e_, h_)])])
        qw_w1[(g, e_, h_)] = last
        _op("scalar", lambda s_: s_.nop())

    def load_w2(g, e_, h_):
        last = DMA("scalar", QW, w2b[:].rearrange("p (fc n) -> p fc n", fc=16),
                   w2s[e_, h_ * 2048:(h_ + 1) * 2048, :]
                   .rearrange("(fc p) n -> p fc n", p=128),
                   waits=[pe_last_w2b_use, (SC, sc_w2[(e_, h_)])])
        qw_w2[(g, e_, h_)] = last
        _op("scalar", lambda s_: s_.nop())

    load_attn_qu(0)
    load_attn_wo(0)

    for g in range(4):
        sl, hf = g // 2, g % 2
        tb = sl * SEQ + hf * 1024
        qub = w1b
        wob = w2b
        iot = iotabc
        ge_iotabc = ge_iota
        ve_iot = None
        if g > 0:
            shift = 1024.0 if hf == 1 else -1024.0
            V(lambda v, shift=shift: v.tensor_scalar(
                iotabc[:], iotabc[:], shift, None, ALU.add),
              waits=[(GE, ge_iota), lastuse.get(("iotabc",))], inc=VE)
            ve_iot = VE.n

        def p2_tables(nt, m):
            s2 = m % 2
            sinb = tbsin[:, s2 * 512:(s2 + 1) * 512]
            cosb = tbcos[:, s2 * 512:(s2 + 1) * 512]
            sw = tabslot_use.get(("p2", s2))
            if m >= 3:
                V(lambda v, nt=nt, m=m: v.tensor_scalar(
                    fsc2[:, 0:512], iot[:, nt * 512: nt * 512 + 512],
                    thcol[:, m:m + 1], None, ALU.mult),
                  waits=[(GE, ge_iotabc), (VE, ve_iot) if ve_iot else None,
                         (QS, consts_q), lastuse.get(("tbA2",))])
                V(lambda v: v.tensor_scalar(fsc2[:, 512:1024], fsc2[:, 0:512],
                                            -1.0, None, ALU.mult), inc=VE)
                ve_a = VE.n
                ACT(sinb, fsc2[:, 0:512], AF.Sin, waits=[(VE, ve_a), sw])
                ACT(cosb, fsc2[:, 512:1024], AF.Sin, bias=cols[:, 1:2], inc=AE)
                lastuse[("tbA2",)] = (AE, AE.n)
                return sinb, cosb, AE.n
            V(lambda v, nt=nt, m=m: v.tensor_scalar(
                tbarg[:, 0:512], iot[:, nt * 512: nt * 512 + 512],
                thcol[:, m:m + 1], None, ALU.mult),
              waits=[(GE, ge_iotabc), (VE, ve_iot) if ve_iot else None,
                     (QS, consts_q), lastuse.get(("tbA2",))])
            V(lambda v: v.tensor_scalar(fsc1[:, 0:512], tbarg[:, 0:512], INV2PI,
                                        None, ALU.mult))
            V(lambda v: v.tensor_copy(ki[:], fsc1[:, 0:512]))
            V(lambda v: v.tensor_copy(fsc1[:, 0:512], ki[:]))
            V(lambda v: v.scalar_tensor_tensor(fsc2[:, 0:512], fsc1[:, 0:512],
                                               -TWO_PI, tbarg[:, 0:512],
                                               ALU.mult, ALU.add))
            V(lambda v: v.tensor_scalar(fsc1[:, 0:512], tbarg[:, 0:512], INV2PI,
                                        0.25, ALU.mult, ALU.add))
            V(lambda v: v.tensor_copy(ki[:], fsc1[:, 0:512]))
            V(lambda v: v.tensor_copy(fsc1[:, 0:512], ki[:]))
            V(lambda v: v.scalar_tensor_tensor(fsc2[:, 512:1024], fsc1[:, 0:512],
                                               -TWO_PI, tbarg[:, 0:512],
                                               ALU.mult, ALU.add), inc=VE)
            ve_args = VE.n
            ACT(sinb, fsc2[:, 0:512], AF.Sin, waits=[(VE, ve_args), sw])
            ACT(cosb, fsc2[:, 512:1024], AF.Sin, bias=cols[:, 1:2], inc=AE)
            lastuse[("tbA2",)] = (AE, AE.n)
            return sinb, cosb, AE.n

        q_xnT = DMA("sync", QS, xnT[:].rearrange("p (c n) -> p c n", c=8),
                    xnsT.rearrange("(c p) t -> p c t", p=128)[:, :, tb:tb + 1024],
                    waits=[(XS, max(xstore_q[tc_]
                                    for tc_ in range(sl * 16 + hf * 8,
                                                     sl * 16 + hf * 8 + 8))),
                           lastuse.get(("xnT",))])
        _op("sync", lambda s_: s_.nop())
        q_acc = None
        for i in range(8):
            q_acc = DMA("sync", QS, acc[:, i * 1024:(i + 1) * 1024],
                        x_c[tb + i * 128: tb + i * 128 + 128, :],
                        waits=[(QS, acc_store_q), lastuse.get(("acc",))])

        if debug and g == 0:
            V(lambda v: v.tensor_copy(fsc1[:, 0:64], rw[:]), waits=[(QW, qw_rw)], inc=VE)
            DMA("sync", QS, rw_d[:], fsc1[:, 0:64], waits=[(VE, VE.n)])
            DMA("sync", QS, xnt_d[:], xnT[:], waits=[(QS, q_xnT)])
            DMA("sync", QS, iob_d[:], iotabc[:], waits=[(GE, ge_iotabc)])
        ve_A = None
        def emit_router(i):
            nonlocal ve_A, prev_ae_ep
            bank = pb[2]
            for c in range(8):
                MM(bank[:, 0:8], xnT[:, c * 1024 + i * 128: c * 1024 + i * 128 + 128],
                   rw[:, c * 8:(c + 1) * 8], c == 0, c == 7,
                   waits=([(QS, q_xnT), (QW, qw_rw), bfree(bank)] if c == 0 else []),
                   inc=(PE if c == 7 else None))
            pe_r = PE.n
            V(lambda v, b=bank: v.tensor_copy(r8a[:], b[:, 0:8]),
              waits=[(PE, pe_r), (AE, prev_ae_ep)], inc=VE)
            bset(bank, (VE, VE.n))
            dbg_cap = debug and g == 0 and i == 0
            if dbg_cap:
                V(lambda v: v.tensor_copy(attsb[:, 0:8], r8a[:]))
            V(lambda v: v.reduce_max(cols[:, 6:7], r8a[:], AX, negate=True), inc=VE)
            ve_n = VE.n
            ACT(r8b[:], r8a[:], AF.Exp, bias=cols[:, 6:7], waits=[(VE, ve_n)], inc=AE)
            ae_z = AE.n
            if dbg_cap:
                V(lambda v: v.tensor_copy(attsb[:, 8:16], r8b[:]),
                  waits=[(AE, ae_z)])
            V(lambda v: v.reduce_sum(cols[:, 6:7], r8b[:], AX),
              waits=[(AE, ae_z)], inc=VE)
            ve_z = VE.n
            V(lambda v: v.reciprocal(cols[:, 7:8], cols[:, 6:7]),
              waits=[(VE, ve_z)], inc=VE)
            ve_rz = VE.n
            V(lambda v: v.tensor_scalar(r8a[:], r8b[:], cols[:, 7:8], None, ALU.mult),
              waits=[(VE, ve_rz)])
            if dbg_cap:
                V(lambda v: v.tensor_copy(attsb[:, 16:24], r8a[:]))
            V(lambda v: v.reduce_max(cols[:, 8:9], r8a[:], AX), inc=VE)
            ve_pm = VE.n
            V(lambda v: v.tensor_scalar(r8c[:], r8a[:], cols[:, 8:9], None, ALU.is_ge),
              waits=[(VE, ve_pm)])
            V(lambda v: v.scalar_tensor_tensor(r8b[:], r8c[:], -2.0, r8a[:],
                                               ALU.mult, ALU.add))
            V(lambda v: v.reduce_max(cols[:, 9:10], r8b[:], AX), inc=VE)
            ve_pm2 = VE.n
            V(lambda v: v.tensor_scalar(r8c[:], r8a[:], cols[:, 9:10], None, ALU.is_ge),
              waits=[(VE, ve_pm2)], inc=VE)
            ve_m2 = VE.n
            if dbg_cap:
                V(lambda v: v.tensor_copy(attsb[:, 24:32], r8c[:]),
                  waits=[(VE, ve_m2)])
            ACT(r8b[:], r8a[:], AF.Exp, waits=[(VE, ve_m2)], inc=AE)
            ae_ep = AE.n
            prev_ae_ep = ae_ep
            V(lambda v: v.tensor_mul(r8a[:], r8b[:], r8c[:]), waits=[(AE, ae_ep)])
            V(lambda v: v.reduce_sum(cols[:, 6:7], r8a[:], AX), inc=VE)
            ve_s2 = VE.n
            V(lambda v: v.reciprocal(cols[:, 10:11], cols[:, 6:7]),
              waits=[(VE, ve_s2)], inc=VE)
            ve_rs2 = VE.n
            V(lambda v: v.tensor_scalar(r8b[:], r8a[:], cols[:, 10:11], None, ALU.mult),
              waits=[(VE, ve_rs2)])
            if dbg_cap:
                V(lambda v: v.tensor_copy(attsb[:, 32:40], r8a[:]))
                V(lambda v: v.tensor_copy(attsb[:, 40:48], r8b[:]))
                V(lambda v: v.tensor_copy(attsb[:, 48:56], sbc[:]))
                V(lambda v: v.tensor_copy(attsb[:, 56:64], cols[:, 8:16][:, 0:8]),
                  inc=VE)
                DMA("sync", QS, rchain_d[:], attsb[:, 0:64], waits=[(VE, VE.n)])
            ve_A = V(lambda v, i=i: v.tensor_mul(Asb[:, i * 8:(i + 1) * 8],
                                                 r8b[:], sbc[:]),
                     waits=[(VE, ve_sbc)], inc=VE)

        for nt in range(2):
            tabnext = p2_tables(nt, 0)
            for m in range(8):
                sinb2, cosb2, ae_tab2 = tabnext
                bank = pb[m % 2]
                for c in range(8):
                    MM(bank[:, 0:512],
                       qub[:, c * 1024 + m * 128: c * 1024 + m * 128 + 128],
                       xnT[:, c * 1024 + nt * 512: c * 1024 + nt * 512 + 512],
                       c == 0, False,
                       waits=([(QW, qw_attnw[(g, "qu")]), (QS, q_xnT), bfree(bank)]
                              if c == 0 else []))
                MM(bank[:, 0:512], bq_row[0:1, m * 128: m * 128 + 128],
                   ones_bf[0:1, 0:512], False, True, inc=PE)
                pe_q = PE.n
                if nt == 0:
                    emit_router(m)
                qsl = m % 2
                ksl2 = ksb[:, qsl * 512:(qsl + 1) * 512]
                ACT(ksl2, bank[:, 0:512], AF.Silu,
                    waits=[(PE, pe_q), lastuse.get(("qs", qsl))], inc=AE)
                bset(bank, (AE, AE.n))
                ae_q = AE.n
                V(lambda v, cb=cosb2, ksl2=ksl2: v.tensor_mul(
                    fsc1[:, 0:512], ksl2, cb),
                  waits=[(AE, max(ae_tab2, ae_q))])
                V(lambda v, sb=sinb2, ksl2=ksl2: v.tensor_mul(
                    fsc2[:, 0:512], ksl2, sb))
                for hh in range(2):
                    h = 2 * m + hh
                    V(lambda v, h=h, hh=hh: v.tensor_copy(
                        big1[0:64, h * 512:(h + 1) * 512],
                        fsc1[hh * 64:hh * 64 + 64, 0:512]),
                      waits=([lastuse.get(("big1",))] if hh == 0 else []))
                    V(lambda v, h=h, hh=hh: v.tensor_copy(
                        big1[64:128, h * 512:(h + 1) * 512],
                        fsc2[hh * 64:hh * 64 + 64, 0:512]),
                      inc=(VE if hh == 1 else None))
                s2 = m % 2
                tabslot_use[("p2", s2)] = (VE, VE.n)
                lastuse[("qs", qsl)] = (VE, VE.n)
                if m + 1 < 8:
                    tabnext = p2_tables(nt, m + 1)
            ve_qrot = VE.n
            for tsub in range(4):
                for dhh in range(2):
                    bank = pb[dhh]
                    for c in range(8):
                        MM(bank[:, 0:512],
                           xnT[:, c * 1024 + nt * 512 + tsub * 128:
                               c * 1024 + nt * 512 + tsub * 128 + 128],
                           qub[:, 8192 + c * 1024 + dhh * 512:
                               8192 + c * 1024 + dhh * 512 + 512],
                           c == 0, False, waits=([bfree(bank)] if c == 0 else []))
                    MM(bank[:, 0:512], ones_bf[0:1, 0:128],
                       bq_row[0:1, 3 * D + dhh * 512: 3 * D + dhh * 512 + 512],
                       False, True, inc=PE)
                    pe_u = PE.n
                    ACT(usb[:, tsub * 1024 + dhh * 512:
                            tsub * 1024 + dhh * 512 + 512],
                        bank[:, 0:512], AF.Copy,
                        waits=[(PE, pe_u), lastuse.get(("usb", tsub))], inc=AE)
                    bset(bank, (AE, AE.n))
            ae_u = AE.n
            for i2 in range(4):
                i = nt * 4 + i2
                ab = (pb[5], pb[6])
                for h in range(NH):
                    bank = ab[h // 8]
                    w = []
                    if h == 0:
                        w = [(VE, ve_qrot), (AE, ae_kv), bfree(ab[0]), bfree(ab[1])]
                    MM(bank[:, (h % 8) * 64:(h % 8) * 64 + 64],
                       big1[:, h * 512 + i2 * 128: h * 512 + i2 * 128 + 128],
                       kvb[:, sl * 1024 + h * 64: sl * 1024 + h * 64 + 64],
                       True, True, waits=w,
                       inc=(PE if h == NH - 1 else None))
                pe_att = PE.n
                slot = i % 2
                st = stats[:, slot * 12:(slot + 1) * 12]
                mvs = mv[:, slot * 2:(slot + 1) * 2]
                V(lambda v, st=st: v.bn_stats(st[:, 0:6], pb[5][:, 0:512]),
                  waits=[(PE, pe_att)])
                V(lambda v, st=st: v.bn_stats(st[:, 6:12], pb[6][:, 0:512]), inc=VE)
                ve_bs2 = VE.n
                V(lambda v, st=st, mvs=mvs: v.bn_aggr(mvs, st),
                  waits=[(VE, ve_bs2)], inc=VE)
                ve_mv = VE.n
                rsd = cols[:, 2 + slot:3 + slot]
                rsd2 = cols[:, 4 + slot:5 + slot]
                ACT(rsd, mvs[:, 1:2], AF.Sqrt, bias=cols[:, 0:1],
                    waits=[(VE, ve_mv)], inc=AE)
                ae_sq = AE.n
                V(lambda v, rsd=rsd, rsd2=rsd2: v.reciprocal(rsd2, rsd),
                  waits=[(AE, ae_sq)], inc=VE)
                ve_rs = VE.n
                V(lambda v, mvs=mvs, rsd2=rsd2: v.tensor_scalar(
                    fsc1[:, 0:512], pb[5][:, 0:512], mvs[:, 0:1], rsd2,
                    ALU.subtract, ALU.mult),
                  waits=[(VE, ve_rs)])
                V(lambda v, mvs=mvs, rsd2=rsd2: v.tensor_scalar(
                    fsc1[:, 512:1024], pb[6][:, 0:512], mvs[:, 0:1], rsd2,
                    ALU.subtract, ALU.mult), inc=VE)
                bset(ab[0], (VE, VE.n))
                bset(ab[1], (VE, VE.n))
                V(lambda v: v.tensor_mul(fsc2[:], fsc1[:], lnwbc[:]),
                  waits=[(VE, ve_lnwbc), lastuse.get(("tbA2",))])
                V(lambda v: v.tensor_add(fsc1[:], fsc2[:], lnbbc[:]),
                  waits=[(VE, ve_lnbbc)])
                gsl = gbf[:, slot * 1024:(slot + 1) * 1024]
                V(lambda v, i2=i2, gsl=gsl: v.tensor_mul(
                    gsl, fsc1[:], usb[:, i2 * 1024:(i2 + 1) * 1024]),
                  waits=[(AE, ae_u), lastuse.get(("gbf", slot))], inc=VE)
                ve_g = VE.n
                lastuse[("usb", i2)] = (VE, ve_g)
                for c in range(8):
                    TRANS(ptb[:, c * 128:(c + 1) * 128],
                          gsl[:, c * 128:(c + 1) * 128],
                          waits=([(VE, ve_g), (QS, consts_q),
                                  lastuse.get(("ptb",))] if c == 0 else []),
                          inc=(PE if c == 7 else None))
                pe_tr = PE.n
                gts = gT[:, slot * 1024:(slot + 1) * 1024]
                V(lambda v, gts=gts: v.tensor_copy(gts, ptb[:]),
                  waits=[(PE, pe_tr), lastuse.get(("gT", slot))], inc=VE)
                ve_gt = VE.n
                lastuse[("ptb",)] = (VE, ve_gt)
                lastuse[("gbf", slot)] = (PE, pe_tr)
                for dhh in range(2):
                    bank = pb[dhh]
                    for c in range(8):
                        MM(bank[:, 0:512], gts[:, c * 128:(c + 1) * 128],
                           wob[:, c * 1024 + dhh * 512: c * 1024 + dhh * 512 + 512],
                           c == 0, False,
                           waits=([(VE, ve_gt), (QW, qw_attnw[(g, "wo")]),
                                   bfree(bank)] if c == 0 else []))
                    MM(bank[:, 0:512], ones_bf[0:1, 0:128],
                       bo_row[0:1, dhh * 512: dhh * 512 + 512], False, True, inc=PE)
                    pe_wo = PE.n
                    aslice = acc[:, i * 1024 + dhh * 512: i * 1024 + dhh * 512 + 512]
                    V(lambda v, b=bank, aslice=aslice: v.scalar_tensor_tensor(
                        aslice, b[:, 0:512], 1.0, aslice, ALU.mult, ALU.add),
                      waits=[(PE, pe_wo), (QS, q_acc)], inc=VE)
                    bset(bank, (VE, VE.n))
                    acc_last_ve[i * 2 + dhh] = VE.n
                lastuse[("gT", slot)] = (PE, PE.n)
            lastuse[("big1",)] = (PE, PE.n)
        if debug and g == 0:
            DMA("sync", QS, A_d[:], Asb[:], waits=[(VE, ve_A)])
            for i in range(8):
                DMA("sync", QS, att_d[:, i * 1024:(i + 1) * 1024],
                    acc[:, i * 1024:(i + 1) * 1024],
                    waits=[(VE, max(acc_last_ve[i * 2], acc_last_ve[i * 2 + 1]))])
            V(lambda v: v.tensor_copy(attsb[:], kvb[:]), waits=[(AE, ae_kv)], inc=VE)
            DMA("sync", QS, kv_d[:], attsb[:], waits=[(VE, VE.n)])
            V(lambda v: v.tensor_copy(attsb[:], big1[:, 0:2048]), inc=VE)
            DMA("sync", QS, q_d[:, 0:2048], attsb[:], waits=[(VE, VE.n)])
        pe_attn_done = PE.n
        pe_last_w1b_use = (PE, pe_attn_done)
        pe_last_w2b_use = (PE, pe_attn_done)
        load_w1(g, 0, 0)
        load_w2(g, 0, 0)

        for e_ in range(E):
            for h_ in range(2):
                for tt in range(2):
                    sil_ae = []
                    for fc in range(16):
                        bank = pb[3 + fc % 2]
                        for c in range(8):
                            MM(bank[:, 0:512],
                               w1b[:, c * 2048 + fc * 128: c * 2048 + fc * 128 + 128],
                               xnT[:, c * 1024 + tt * 512: c * 1024 + tt * 512 + 512],
                               c == 0, c == 7,
                               waits=([(QW, qw_w1[(g, e_, h_)]), (QS, q_xnT),
                                       bfree(bank)] if c == 0 else []),
                               inc=(PE if c == 7 else None))
                        pe_h = PE.n
                        ACT(big1[:, fc * 512:(fc + 1) * 512], bank[:, 0:512], AF.Silu,
                            waits=[(PE, pe_h), lastuse.get(("big1",))], inc=AE)
                        bset(bank, (AE, AE.n))
                        sil_ae.append(AE.n)
                    ae_hdone = AE.n
                    pe_H_last = PE.n
                    if tt == 1:
                        ne, nh = (e_, 1) if h_ == 0 else (e_ + 1, 0)
                        pe_last_w1b_use = (PE, pe_H_last)
                        if ne < E:
                            load_w1(g, ne, nh)
                        elif g < 3:
                            load_attn_qu(g + 1)
                        if ne >= E:
                            lastuse[("xnT",)] = (PE, pe_H_last)
                    for tsub in range(4):
                        i = tt * 4 + tsub
                        for dhh in range(2):
                            bank = pb[5 + dhh]
                            for fc in range(16):
                                w = [(AE, sil_ae[fc])]
                                if fc == 0:
                                    w += [(QW, qw_w2[(g, e_, h_)]), bfree(bank)]
                                MM(bank[:, 0:512],
                                   big1[:, fc * 512 + tsub * 128:
                                        fc * 512 + tsub * 128 + 128],
                                   w2b[:, fc * 1024 + dhh * 512:
                                       fc * 1024 + dhh * 512 + 512],
                                   fc == 0, fc == 15,
                                   waits=w,
                                   inc=(PE if fc == 15 else None))
                            pe_y = PE.n
                            aslice = acc[:, i * 1024 + dhh * 512:
                                         i * 1024 + dhh * 512 + 512]
                            acol = Asb[:, i * 8 + e_: i * 8 + e_ + 1]
                            V(lambda v, b=bank, aslice=aslice, acol=acol:
                              v.scalar_tensor_tensor(aslice, b[:, 0:512], acol,
                                                     aslice, ALU.mult, ALU.add),
                              waits=[(PE, pe_y), (VE, ve_A)], inc=VE)
                            bset(bank, (VE, VE.n))
                            acc_last_ve[i * 2 + dhh] = VE.n
                    lastuse[("big1",)] = (PE, PE.n)
                ne, nh = (e_, 1) if h_ == 0 else (e_ + 1, 0)
                pe_last_w2b_use = (PE, PE.n)
                if ne < E:
                    load_w2(g, ne, nh)
                elif g < 3:
                    load_attn_wo(g + 1)

        for i in range(8):
            acc_store_q = DMA(
                "sync", QS, out_c[tb + i * 128: tb + i * 128 + 128, :],
                acc[:, i * 1024:(i + 1) * 1024],
                waits=[(VE, max(acc_last_ve[i * 2], acc_last_ve[i * 2 + 1]))])

    _op("sync", lambda s_: s_.nop(), waits=[(QS, QS.n)])

    with nc.Block() as block:
        @block.sync
        def _(sync):
            for f in P["sync"]:
                f(sync)

        @block.scalar
        def _(scalar):
            for f in P["scalar"]:
                f(scalar)

        @block.vector
        def _(vector):
            for f in P["vector"]:
                f(vector)

        @block.tensor
        def _(tensor):
            for f in P["tensor"]:
                f(tensor)

        @block.gpsimd
        def _(gpsimd):
            for f in P["gpsimd"]:
                f(gpsimd)

    es.close()
    return nc


_CACHE = {}


def kernel(**inputs):
    _ = inputs.get("L0_attn_context")
    x = np.ascontiguousarray(np.asarray(inputs["L0_attn_x"], dtype=np.float32))
    wq = np.ascontiguousarray(np.asarray(inputs["L0_attn_Wqkvu"],
                                         dtype=np.float32).T)
    bq = np.ascontiguousarray(np.asarray(inputs["L0_attn_bqkvu"],
                                         dtype=np.float32)).reshape(1, -1)
    wo = np.ascontiguousarray(np.asarray(inputs["L0_attn_Wout"],
                                         dtype=np.float32).T)
    bo = np.ascontiguousarray(np.asarray(inputs["L0_attn_bout"],
                                         dtype=np.float32)).reshape(1, -1)
    theta = np.ascontiguousarray(np.asarray(inputs["L0_attn_theta"],
                                            dtype=np.float32)).reshape(-1)
    lnw = np.ascontiguousarray(np.asarray(inputs["L0_attn_ln_w"],
                                          dtype=np.float32)).reshape(1, -1)
    lnb = np.ascontiguousarray(np.asarray(inputs["L0_attn_ln_b"],
                                          dtype=np.float32)).reshape(1, -1)
    rwm = np.ascontiguousarray(np.asarray(inputs["L0_ffn_router_w"], dtype=np.float32))
    w1 = np.ascontiguousarray(np.asarray(inputs["L0_ffn_w1"],
                                         dtype=np.float32).transpose(0, 2, 1))
    w2 = np.ascontiguousarray(np.asarray(inputs["L0_ffn_w2"],
                                         dtype=np.float32).transpose(0, 2, 1))
    esc = np.ascontiguousarray(np.asarray(inputs["L0_ffn_expert_scale"],
                                          dtype=np.float32)).reshape(1, -1)

    th_row = np.ascontiguousarray(theta.reshape(1, -1))
    thcol = np.ascontiguousarray(theta.reshape(8, 128).T)
    rwT = np.ascontiguousarray(rwm.T)
    eye = np.eye(128, dtype=ml_dtypes.bfloat16)

    if "nc" not in _CACHE:
        _CACHE["nc"] = build_nc()
    nc = _CACHE["nc"]

    in_maps = []
    for c in range(NCORES):
        xc = np.ascontiguousarray(x[2 * c:2 * c + 2].reshape(T, D))
        in_maps.append({
            "x_c": xc, "wq_f": wq, "wout_f": wo, "rw_f": rwT,
            "w1_f": w1, "w2_f": w2, "bq_row": bq, "bo_row": bo,
            "th_row": th_row, "thcol": thcol, "lnw_row": lnw, "lnb_row": lnb,
            "sc_row": esc, "eye_d": eye,
        })
    res = run_bass_kernel_spmd(nc, in_maps, core_ids=list(range(NCORES)))
    out = np.empty((16, SEQ, D), dtype=np.float32)
    for c in range(NCORES):
        out[2 * c:2 * c + 2] = res.results[c]["out_c"].reshape(2, SEQ, D)
    return out
